# version 04
# baseline (speedup 1.0000x reference)
import numpy as np
from math import comb

import concourse.bass as bass
import concourse.mybir as mybir
from concourse.bass_utils import run_bass_kernel_spmd

F32 = mybir.dt.float32
F32R = mybir.dt.float32r
BF16 = mybir.dt.bfloat16
I8 = mybir.dt.int8
I32 = mybir.dt.int32
AF = mybir.ActivationFunctionType
ALU = mybir.AluOpType

D = 2048
NTOK = 16384
NCORE = 8
PTOK = NTOK // NCORE
T = 512
NTILE = PTOK // T
KC = D // 128
HEADS = 16
INTER = 8192
NI = INTER // 128
NCTRL = 130156
NS = 16777216
NBLK = NS // 128
LN_EPS = 1e-5

ENGINES = ("tensor", "vector", "scalar", "sync", "gpsimd")

_POLY = {
    0: np.array([0.0, -0.5, 1.0, -0.5]),
    1: np.array([1.0, 0.0, -2.5, 1.5]),
    2: np.array([0.0, 0.5, 2.0, -1.5]),
    3: np.array([0.0, 0.0, -0.5, 0.5]),
}
_CACHE = {}


def _tables():
    if "t" in _CACHE:
        return _CACHE["t"]
    t = np.linspace(0.0, NCTRL - 1.0, NS, dtype=np.float64)
    i_all = np.clip(np.floor(t).astype(np.int64), 0, NCTRL - 2)
    step = (NCTRL - 1.0) / (NS - 1.0)
    iA = i_all[0::128].copy()
    phi = t[0::128] - iA
    i_blk = i_all.reshape(NBLK, 128)
    crossed = i_blk > iA[:, None]
    sc = np.where(crossed.any(axis=1), crossed.argmax(axis=1), 128).astype(np.int64)

    s = np.arange(128, dtype=np.float64)
    sdt = s * step
    UA = np.zeros((20, 128))
    UB = np.zeros((20, 128))
    for l in range(4):
        for ti, tau in enumerate((-1, 0, 1, 2, 3)):
            row = 4 * ti + l
            if -1 <= tau <= 2:
                pa = _POLY[tau + 1]
                UA[row] = sum(pa[j] * comb(j, l) * sdt ** (j - l) for j in range(l, 4))
            if 0 <= tau <= 3:
                pb = _POLY[tau]
                UB[row] = sum(pb[j] * comb(j, l) * (sdt - 1.0) ** (j - l) for j in range(l, 4))

    bp = np.arange(NBLK, dtype=np.int64)
    b_of_gu = 16 * (128 * (bp // 2048) + (bp % 128)) + (bp % 2048) // 128
    b_of_dn = 64 * (128 * (bp // 8192) + (bp % 128)) + (bp % 8192) // 128

    phi_pows = np.stack([phi ** l for l in range(4)], axis=0)
    PHI20 = np.concatenate([phi_pows] * 5, axis=0)
    s_idx = np.arange(128)[:, None]
    msk = (s_idx >= sc[None, :]).astype(np.int8)

    res = dict(
        iA=iA,
        UA32=UA.astype(np.float32), UB32=UB.astype(np.float32),
        b_of_gu=b_of_gu, b_of_dn=b_of_dn,
        PHI_gu=np.ascontiguousarray(PHI20[:, b_of_gu]).astype(np.float32),
        PHI_dn=np.ascontiguousarray(PHI20[:, b_of_dn]).astype(np.float32),
        MSK_gu=np.ascontiguousarray(msk[:, b_of_gu]),
        MSK_dn=np.ascontiguousarray(msk[:, b_of_dn]),
    )
    _CACHE["t"] = res
    return res


def _cpt(cp, perm_b_of):
    tb = _tables()
    iA = tb["iA"][perm_b_of]
    rows = []
    for tau in (-1, 0, 1, 2, 3):
        v = np.take(cp, np.clip(iA + tau, 0, NCTRL - 1))
        rows.extend([v, v, v, v])
    return np.ascontiguousarray(np.stack(rows, axis=0).astype(np.float32))


class Ctr:
    __slots__ = ("name", "n")

    def __init__(self, name):
        self.name = name
        self.n = 0


class Sched:
    def __init__(self):
        self.ops = {e: [] for e in ENGINES}

    def add(self, eng, fn, waits=(), inc=None, k=1):
        after = None
        if inc is not None:
            inc.n += k
            after = (inc, k)
        ws = tuple((c, v) for (c, v) in waits if v and v > 0)
        self.ops[eng].append((ws, fn, after))
        return inc.n if inc is not None else None

    def replay(self, eng_name, eng, sems):
        for ws, fn, after in self.ops[eng_name]:
            for c, v in ws:
                eng.wait_ge(sems[c.name], v)
            instr = fn(eng)
            if after is not None:
                c, k = after
                assert instr is not None
                instr.then_inc(sems[c.name], k)


def _build(debug_outputs=False):
    nc = bass.Bass("TRN2", target_bir_lowering=False, debug=False)

    xin = nc.dram_tensor("x", [PTOK, D], F32, kind="ExternalInput").ap()
    wqk = nc.dram_tensor("wqk", [D, D], F32, kind="ExternalInput").ap()
    wv = nc.dram_tensor("wv", [D, D], F32, kind="ExternalInput").ap()
    wu = nc.dram_tensor("wu", [D, D], F32, kind="ExternalInput").ap()
    wo = nc.dram_tensor("wo", [D, D], F32, kind="ExternalInput").ap()
    biases = nc.dram_tensor("biases", [128, 6 * 16], F32, kind="ExternalInput").ap()
    cptg = nc.dram_tensor("cptg", [20, NBLK], F32, kind="ExternalInput").ap()
    cptu = nc.dram_tensor("cptu", [20, NBLK], F32, kind="ExternalInput").ap()
    cptd = nc.dram_tensor("cptd", [20, NBLK], F32, kind="ExternalInput").ap()
    phig = nc.dram_tensor("phig", [20, NBLK], F32, kind="ExternalInput").ap()
    phid = nc.dram_tensor("phid", [20, NBLK], F32, kind="ExternalInput").ap()
    mskg = nc.dram_tensor("mskg", [128, NBLK], I8, kind="ExternalInput").ap()
    mskd = nc.dram_tensor("mskd", [128, NBLK], I8, kind="ExternalInput").ap()
    basis = nc.dram_tensor("basis", [32, 512], F32R, kind="ExternalInput").ap()
    onesr_d = nc.dram_tensor("onesr_d", [1, 128], F32R, kind="ExternalInput").ap()

    yout = nc.dram_tensor("y", [PTOK, D], F32, kind="ExternalOutput").ap()
    if debug_outputs:
        w2g = nc.dram_tensor("w2g", [128, NBLK], BF16, kind="ExternalOutput").ap()
        w2u = nc.dram_tensor("w2u", [128, NBLK], BF16, kind="ExternalOutput").ap()
        w2d = nc.dram_tensor("w2d", [128, NBLK], BF16, kind="ExternalOutput").ap()
    else:
        w2g = nc.dram_tensor("w2g", [128, NBLK], BF16, kind="Internal").ap()
        w2u = nc.dram_tensor("w2u", [128, NBLK], BF16, kind="Internal").ap()
        w2d = nc.dram_tensor("w2d", [128, NBLK], BF16, kind="Internal").ap()

    if debug_outputs:
        dbg1 = nc.dram_tensor("dbg1", [128, KC * PTOK], F32, kind="ExternalOutput").ap()
        dbg2 = nc.dram_tensor("dbg2", [128, KC * PTOK], F32, kind="ExternalOutput").ap()
        dbg_xa = nc.dram_tensor("dbg_xa", [128, KC * T], BF16, kind="ExternalOutput").ap()
        dbg_dn = nc.dram_tensor("dbg_dn", [128, T], F32, kind="ExternalOutput").ap()
        dbg_int = nc.dram_tensor("dbg_int", [128, NI * T], BF16, kind="ExternalOutput").ap()
    x_fm = nc.dram_tensor("x_fm", [128, KC * PTOK], F32, kind="Internal").ap()
    wqk_b = nc.dram_tensor("wqk_b", [128, KC * D], BF16, kind="Internal").ap()
    wv_b = nc.dram_tensor("wv_b", [128, KC * D], BF16, kind="Internal").ap()
    wu_b = nc.dram_tensor("wu_b", [128, KC * D], BF16, kind="Internal").ap()
    wo_b = nc.dram_tensor("wo_b", [128, KC * D], BF16, kind="Internal").ap()
    _dk = "ExternalOutput" if debug_outputs else "Internal"
    qk_fm = nc.dram_tensor("qk_fm", [128, KC * PTOK], BF16, kind=_dk).ap()
    u_fm = nc.dram_tensor("u_fm", [128, KC * PTOK], BF16, kind=_dk).ap()
    kv_in = nc.dram_tensor("kv_in", [128, HEADS * 128], F32, kind="Internal").ap()
    kv_out = nc.dram_tensor("kv_out", [128, HEADS * 128], F32, kind="Internal").ap()
    if debug_outputs:
        dbg_kv = nc.dram_tensor("dbg_kv", [128, HEADS * 128], F32, kind="ExternalOutput").ap()
        dbg_kvp = nc.dram_tensor("dbg_kvp", [128, HEADS * 128], F32, kind="ExternalOutput").ap()

    sch = Sched()
    cSY = Ctr("sy")
    cSC = Ctr("sc")
    cGP = Ctr("gp")
    cPE = Ctr("pe")
    cDV = Ctr("dv")
    cAC = Ctr("ac")
    cCC = Ctr("cc")
    cX = Ctr("x")
    cXST = Ctr("xst")
    cGW = [Ctr("gw0"), Ctr("gw1")]
    cUW = [Ctr("uw0"), Ctr("uw1")]
    cDW = [Ctr("dw0"), Ctr("dw1")]
    cAW = [Ctr("aw0"), Ctr("aw1")]
    cQC = [Ctr("qc0"), Ctr("qc1")]
    cUC = [Ctr("uc0"), Ctr("uc1")]
    cSTG = Ctr("stg")
    cMSK = Ctr("msk")
    cW2 = [Ctr("w20"), Ctr("w21")]
    cPC = Ctr("pc")
    cSPL = Ctr("spl")
    cTR = Ctr("tr")
    cKV = Ctr("kv")

    def x_fm_ap(kc, t0, n):
        return x_fm[:].rearrange("p (kc t) -> p kc t", kc=KC)[:, kc, t0:t0 + n]

    def qk_fm_ap(kc, t0, n):
        return qk_fm[:].rearrange("p (kc t) -> p kc t", kc=KC)[:, kc, t0:t0 + n]

    def u_fm_ap(kc, t0, n):
        return u_fm[:].rearrange("p (kc t) -> p kc t", kc=KC)[:, kc, t0:t0 + n]

    import contextlib
    with contextlib.ExitStack() as _stack:
        xt = _stack.enter_context(nc.sbuf_tensor("xt", [128, KC * T], F32))
        xa = _stack.enter_context(nc.sbuf_tensor("xa", [128, KC * T], BF16))
        big = _stack.enter_context(nc.sbuf_tensor("big", [128, NI * T], BF16))
        xbc = _stack.enter_context(nc.sbuf_tensor("xbc", [128, 2 * T], BF16))
        xsqc = _stack.enter_context(nc.sbuf_tensor("xsqc", [128, 2 * T], BF16))
        srowA = _stack.enter_context(nc.sbuf_tensor("srowA", [65, T], F32))
        srowB = _stack.enter_context(nc.sbuf_tensor("srowB", [65, T], F32))
        srr_t = _stack.enter_context(nc.sbuf_tensor("srr", [1, T], F32R))
        bcA = _stack.enter_context(nc.sbuf_tensor("bcA", [128, T], F32))
        bcB = _stack.enter_context(nc.sbuf_tensor("bcB", [128, T], F32))
        gwb = _stack.enter_context(nc.sbuf_tensor("gwb", [128, 2 * D], BF16))
        uwb = _stack.enter_context(nc.sbuf_tensor("uwb", [128, 2 * D], BF16))
        dwb = _stack.enter_context(nc.sbuf_tensor("dwb", [128, 2 * D], BF16))
        awb = _stack.enter_context(nc.sbuf_tensor("awb", [128, 2 * D], BF16))
        tmp16 = _stack.enter_context(nc.sbuf_tensor("tmp16", [128, 2 * T], BF16))
        tmp32 = _stack.enter_context(nc.sbuf_tensor("tmp32", [128, 2 * T], F32))
        zbuf = _stack.enter_context(nc.sbuf_tensor("zbuf", [128, 2 * T], F32))
        m8 = _stack.enter_context(nc.sbuf_tensor("m8", [128, 2 * T], I8))
        stg_c = _stack.enter_context(nc.sbuf_tensor("stg_c", [20, 1024], F32))
        stg_p = _stack.enter_context(nc.sbuf_tensor("stg_p", [20, 1024], F32))
        fsb = _stack.enter_context(nc.sbuf_tensor("fsb", [32, 1024], F32R))
        mskb = _stack.enter_context(nc.sbuf_tensor("mskb", [128, 1024], I8))
        wsb = _stack.enter_context(nc.sbuf_tensor("wsb", [128, 2 * 1024], BF16))
        bas = _stack.enter_context(nc.sbuf_tensor("bas", [32, 512], F32R))
        bias_sb = _stack.enter_context(nc.sbuf_tensor("bias_sb", [128, 6 * 16], F32))
        onesc = _stack.enter_context(nc.sbuf_tensor("onesc", [128, 1], BF16))
        onesr = _stack.enter_context(nc.sbuf_tensor("onesr", [1, 128], F32R))
        ident = _stack.enter_context(nc.sbuf_tensor("ident", [128, 128], F32))
        io1 = _stack.enter_context(nc.sbuf_tensor("io1", [128, 128], I32))
        io2 = _stack.enter_context(nc.sbuf_tensor("io2", [128, 128], I32))
        kvsb = _stack.enter_context(nc.sbuf_tensor("kvsb", [128, HEADS * 128], BF16))
        qch = _stack.enter_context(nc.sbuf_tensor("qch", [128, 2 * T], BF16))
        uch = _stack.enter_context(nc.sbuf_tensor("uch", [128, 2 * T], BF16))
        pg = _stack.enter_context(nc.psum_tensor("pg", [128, 2 * T], F32))
        pu = _stack.enter_context(nc.psum_tensor("pu", [128, 2 * T], F32))
        pkv = _stack.enter_context(nc.psum_tensor("pkv", [128, 4 * T], F32))
        _sem_names = ["sy", "sc", "gp", "pe", "dv", "ac", "cc", "x", "xst",
                      "gw0", "gw1", "uw0", "uw1", "dw0", "dw1", "aw0", "aw1",
                      "qc0", "qc1", "uc0", "uc1", "stg", "msk", "w20", "w21",
                      "pc", "spl", "tr", "kv"]
        sems = {n: _stack.enter_context(nc.semaphore("s_" + n)) for n in _sem_names}
        block = _stack.enter_context(nc.Block())

        pdn = pkv[:, 0:T]
        psq = pkv[:, T:2 * T]
        p_recA = pkv[:, T:2 * T]
        p_recB = pkv[:, 2 * T:3 * T]

        def bias_ap(idx, kc):
            return bias_sb[:, idx * 16 + kc: idx * 16 + kc + 1]

        sch.add("gpsimd", lambda e: e.dma_start(out=bias_sb[:], in_=biases[:]), inc=cGP, k=16)
        sch.add("gpsimd", lambda e: e.dma_start(out=bas[:], in_=basis[:]), inc=cGP, k=16)
        sch.add("gpsimd", lambda e: e.dma_start(out=onesr[:], in_=onesr_d[:]), inc=cGP, k=16)
        v_smalls = cGP.n
        sch.add("gpsimd", lambda e: e.memset(onesc[:], 1.0), inc=cGP, k=1)
        v_onesc = cGP.n
        sch.add("gpsimd", lambda e: e.iota(io1[:], pattern=[[1, 128]], base=0, channel_multiplier=0), inc=cGP, k=1)
        sch.add("gpsimd", lambda e: e.iota(io2[:], pattern=[[0, 128]], base=0, channel_multiplier=1), inc=cGP, k=1)
        v_io = cGP.n
        sch.add("vector", lambda e: e.tensor_tensor(ident[:], io1[:], io2[:], ALU.is_equal),
                waits=[(cGP, v_io)], inc=cDV)
        ident_done = cDV.n
        for (wsrc, wdst) in ((wqk, wqk_b), (wv, wv_b), (wu, wu_b), (wo, wo_b)):
            for oc in range(KC):
                src_r = wsrc.rearrange("(k p) (oc o2) -> p oc k o2", p=128, o2=128)[:, oc]
                dst_r = wdst[:].rearrange("p (oc k o2) -> p oc k o2", k=KC, o2=128)[:, oc]
                sch.add("gpsimd", lambda e, s=src_r, d=dst_r: e.dma_start(out=d, in_=s),
                        inc=cPC, k=16)
        v_precast = cPC.n

        pe_done_tc = {}
        dv_done_round = {}
        sy_store_round = {}
        n_tc = PTOK // 128

        def prelude_tc(tc):
                half = tc % 2
                stage = xt[:, half * 2048: half * 2048 + 2048]
                w = [(cPE, pe_done_tc.get(tc - 2))] if tc >= 2 else []
                sch.add("sync", lambda e, st=stage, tc=tc: e.dma_start(
                    out=st, in_=xin[tc * 128:(tc + 1) * 128, :]), waits=w, inc=cSY, k=16)
                v_ld = cSY.n
                for g4 in range(4):
                    rnd = tc * 4 + g4
                    pb = pkv[:, 0:T] if rnd % 2 == 0 else pkv[:, 3 * T:4 * T]
                    wts = [(cSY, v_ld), (cDV, ident_done)]
                    if rnd >= 2:
                        wts.append((cDV, dv_done_round[rnd - 2]))
                    for j in range(4):
                        kc = g4 * 4 + j
                        sch.add("tensor",
                                lambda e, pb=pb, st=stage, kc=kc, j=j: e.transpose(
                                    pb[:, j * 128:(j + 1) * 128],
                                    st[:, kc * 128:(kc + 1) * 128], ident[:]),
                                waits=wts if j == 0 else (),
                                inc=(cPE if j == 3 else None))
                    v_pe = cPE.n
                    if g4 == 3:
                        pe_done_tc[tc] = v_pe
                    h2 = rnd % 2
                    w_cp = [(cPE, v_pe)]
                    if sy_store_round.get(rnd - 1):
                        w_cp.append((cXST, sy_store_round[rnd - 1]))
                    sch.add("vector", lambda e, pb=pb, h2=h2: e.tensor_copy(
                        tmp32[:, h2 * T:(h2 + 1) * T], pb), waits=w_cp, inc=cDV)
                    dv_done_round[rnd] = cDV.n
                    sch.add("sync", lambda e, tc=tc, g4=g4, h2=h2: e.dma_start(
                        out=x_fm[:].rearrange("p (kc t) -> p kc t", kc=KC)[
                            :, g4 * 4:(g4 + 1) * 4, tc * 128:(tc + 1) * 128],
                        in_=tmp32[:, h2 * T:(h2 + 1) * T].rearrange("p (j t) -> p j t", j=4)),
                        waits=[(cDV, dv_done_round[rnd])], inc=cXST, k=16)
                    sy_store_round[rnd] = cXST.n

        recon_store = {}
        rstate = {"parity": 0, "half_user": {}, "tiles": 0, "last_cp": 0, "last_pred": 0}

        def recon_chunk(wname, cpt_d, phi_d, msk_d, w2_d, basA, basB, ch):
            half = rstate["parity"]
            rstate["parity"] ^= 1
            fsb_v = fsb[:, 0:1024]
            msk_v = mskb[:, 0:1024]
            wsb_v = wsb[:, half * 1024: half * 1024 + 1024]
            sl = slice(ch * 1024, (ch + 1) * 1024)
            prev = rstate["half_user"].get(half)
            last = rstate.get("lastchunk")
            w_ld = [(cDV, last["fbuild"])] if last else []
            sch.add("gpsimd", lambda e, s=stg_c, d=cpt_d, sl=sl: e.dma_start(
                out=s[:], in_=d[:, sl]), waits=w_ld, inc=cSTG, k=16)
            sch.add("gpsimd", lambda e, s=stg_p, d=phi_d, sl=sl: e.dma_start(
                out=s[:], in_=d[:, sl]), inc=cSTG, k=16)
            v_stg = cSTG.n
            w_msk = [(cDV, last["lastpred"])] if last else []
            sch.add("scalar", lambda e, m=msk_v, d=msk_d, sl=sl: e.dma_start(
                out=m, in_=d[:, sl]), waits=w_msk, inc=cMSK, k=16)
            v_msk = cMSK.n
            w_fb = [(cSTG, v_stg)]
            if last:
                w_fb.append((cPE, last["mm"]))
            sch.add("vector", lambda e: e.tensor_tensor(
                fsb_v[0:20, :], stg_c[:], stg_p[:], ALU.mult), waits=w_fb, inc=cDV)
            v_fb = cDV.n
            mm_last = None
            for q in range(2):
                tl = slice(q * 512, (q + 1) * 512)
                w_mm = [(cDV, v_fb), (cGP, v_smalls)]
                if rstate["tiles"] > 0:
                    w_mm.append((cAC, rstate["last_cp"]))
                    w_mm.append((cDV, rstate["last_pred"]))
                sch.add("tensor", lambda e, f=fsb_v, tl=tl, bA=basA: e.matmul(
                    p_recA, bA, f[0:20, tl], start=True, stop=True), waits=w_mm)
                sch.add("tensor", lambda e, f=fsb_v, tl=tl, bB=basB: e.matmul(
                    p_recB, bB, f[0:20, tl], start=True, stop=True), inc=cPE)
                v_mm = cPE.n
                mm_last = v_mm
                w_cp = [(cPE, v_mm)]
                if q == 0 and prev:
                    w_cp.append((prev["store_ctr"], prev["store"]))
                sch.add("scalar", lambda e, w=wsb_v, tl=tl: e.copy(w[:, tl], p_recA),
                        waits=w_cp, inc=cAC)
                rstate["last_cp"] = cAC.n
                sch.add("vector", lambda e, w=wsb_v, m=msk_v, tl=tl: e.copy_predicated(
                    w[:, tl], m[:, tl], p_recB),
                    waits=[(cAC, rstate["last_cp"]), (cMSK, v_msk)], inc=cDV)
                rstate["last_pred"] = cDV.n
                rstate["tiles"] += 1
            cw2 = cW2[half]
            w_st = [(cDV, rstate["last_pred"])]
            if cw2.n > 0:
                w_st.append((cw2, cw2.n))
            sch.add("scalar", lambda e, w=wsb_v, d=w2_d, sl=sl: e.dma_start(
                out=d[:, sl], in_=w), waits=w_st, inc=cw2, k=16)
            recon_store[(wname, ch)] = (cw2, cw2.n)
            rec = dict(store_ctr=cw2, store=cw2.n, fbuild=v_fb, mm=mm_last,
                       lastpred=rstate["last_pred"])
            rstate["half_user"][half] = rec
            rstate["lastchunk"] = rec

        basA_gu = bas[0:20, 0:128]
        basB_gu = bas[0:20, 128:256]
        basA_d = bas[0:20, 256:384]
        basB_d = bas[0:20, 384:512]
        recon_jobs = []
        for ic in range(NI):
            recon_jobs.append(("g", cptg, phig, mskg, w2g, basA_gu, basB_gu, 2 * ic))
            recon_jobs.append(("g", cptg, phig, mskg, w2g, basA_gu, basB_gu, 2 * ic + 1))
            recon_jobs.append(("u", cptu, phig, mskg, w2u, basA_gu, basB_gu, 2 * ic))
            recon_jobs.append(("u", cptu, phig, mskg, w2u, basA_gu, basB_gu, 2 * ic + 1))
        for ch in range(2 * NI):
            recon_jobs.append(("d", cptd, phid, mskd, w2d, basA_d, basB_d, ch))
        _rq = {"i": 0}

        def emit_recon(n):
            while n > 0 and _rq["i"] < len(recon_jobs):
                recon_chunk(*recon_jobs[_rq["i"]])
                _rq["i"] += 1
                n -= 1

        for tc in range(n_tc):
            prelude_tc(tc)
            emit_recon(4)
        v_pre_xst = cXST.n

        stats_mm_after = {}

        def ln_stats_xa(chunk_ready, extra_first_mm_waits=()):
            for kc in range(KC):
                h = kc % 2
                w = [(cX, chunk_ready[kc])]
                if kc >= 2:
                    w.append((cPE, stats_mm_after[kc - 2]))
                sch.add("scalar", lambda e, kc=kc, h=h: e.copy(
                    xbc[:, h * T:(h + 1) * T], xt[:, kc * T:(kc + 1) * T]), waits=w, inc=cAC)
                sch.add("scalar", lambda e, h=h: e.activation(
                    out=xsqc[:, h * T:(h + 1) * T], in_=xbc[:, h * T:(h + 1) * T],
                    func=AF.Square), inc=cAC)
                v_sq = cAC.n
                w_mm = [(cAC, v_sq), (cGP, v_onesc)]
                if kc == 0:
                    w_mm += list(extra_first_mm_waits)
                sch.add("tensor", lambda e, kc=kc, h=h: e.matmul(
                    pg[0:1, 0:T], onesc[:], xbc[:, h * T:(h + 1) * T],
                    start=(kc == 0), stop=(kc == KC - 1)), waits=w_mm)
                sch.add("tensor", lambda e, kc=kc, h=h: e.matmul(
                    pu[0:1, 0:T], onesc[:], xsqc[:, h * T:(h + 1) * T],
                    start=(kc == 0), stop=(kc == KC - 1)), inc=cPE)
                stats_mm_after[kc] = cPE.n
            v_sums = cPE.n
            return _row_math(pg[0:1, 0:T], pu[0:1, 0:T], v_sums, cPE)

        def _row_math(sum_ap, sq_ap, v_sums, sums_ctr):
            sch.add("vector", lambda e: e.tensor_scalar(
                srowA[0:1, :], sum_ap, 1.0 / D, 0.0, ALU.mult, ALU.add),
                waits=[(sums_ctr, v_sums)], inc=cDV)
            sch.add("vector", lambda e: e.tensor_scalar(
                srowA[32:33, :], sq_ap, 1.0 / D, LN_EPS, ALU.mult, ALU.add), inc=cDV)
            sch.add("vector", lambda e: e.tensor_tensor(
                srowB[32:33, :], srowA[0:1, :], srowA[0:1, :], ALU.mult), inc=cDV)
            sch.add("vector", lambda e: e.tensor_tensor(
                srowA[32:33, :], srowA[32:33, :], srowB[32:33, :], ALU.subtract), inc=cDV)
            v_var = cDV.n
            sch.add("scalar", lambda e: e.activation(
                out=srowA[64:65, :], in_=srowA[32:33, :], func=AF.Ln),
                waits=[(cDV, v_var)], inc=cAC)
            sch.add("scalar", lambda e: e.activation(
                out=srowB[0:1, :], in_=srowA[64:65, :], func=AF.Exp, scale=-0.5),
                inc=cAC)
            v_rstd = cAC.n
            sch.add("vector", lambda e: e.tensor_tensor(
                srowB[32:33, :], srowA[0:1, :], srowB[0:1, :], ALU.mult),
                waits=[(cAC, v_rstd)], inc=cDV)
            sch.add("vector", lambda e: e.tensor_copy(srr_t[0:1, :], srowB[0:1, :]), inc=cDV)
            v_srr = cDV.n
            sch.add("tensor", lambda e: e.matmul(
                pg[:, 0:T], onesr[:], srr_t[0:1, :], start=True, stop=True),
                waits=[(cDV, v_srr)], inc=cPE)
            v_bc1 = cPE.n
            sch.add("vector", lambda e: e.tensor_copy(srr_t[0:1, :], srowB[32:33, :]),
                    waits=[(cPE, v_bc1)], inc=cDV)
            v_srr2 = cDV.n
            sch.add("tensor", lambda e: e.matmul(
                pu[:, 0:T], onesr[:], srr_t[0:1, :], start=True, stop=True),
                waits=[(cDV, v_srr2)], inc=cPE)
            v_bcmm = cPE.n
            sch.add("vector", lambda e: e.tensor_copy(bcA[:], pg[:, 0:T]),
                    waits=[(cPE, v_bcmm)], inc=cDV)
            sch.add("vector", lambda e: e.tensor_copy(bcB[:], pu[:, 0:T]), inc=cDV)
            return cDV.n

        def make_xa(v_bc):
            for kc in range(KC):
                h = kc % 2
                sch.add("vector", lambda e, kc=kc, h=h: e.tensor_tensor(
                    tmp32[:, h * T:(h + 1) * T], xt[:, kc * T:(kc + 1) * T], bcA[:],
                    ALU.mult), waits=[(cDV, v_bc)] if kc == 0 else ())
                sch.add("vector", lambda e, kc=kc, h=h: e.tensor_tensor(
                    xa[:, kc * T:(kc + 1) * T], tmp32[:, h * T:(h + 1) * T], bcB[:],
                    ALU.subtract), inc=cDV)
            return cDV.n

        def ph_ffn(call, xt_free_sy):
            prev_ystore = None
            for t in range(NTILE):
                for kc in range(KC):
                    w = [(cXST, xt_free_sy)] if kc == 0 and xt_free_sy else []
                    sch.add("sync", lambda e, kc=kc, t=t: e.dma_start(
                        out=xt[:, kc * T:(kc + 1) * T], in_=x_fm_ap(kc, t * T, T)),
                        waits=w, inc=cX, k=16)
                xt_free_sy = None
                chunk_ready = [cX.n] * KC
                v_bc = ln_stats_xa(chunk_ready)
                v_xa = make_xa(v_bc)
                if debug_outputs and call == 0 and t == 0:
                    sch.add("sync", lambda e: e.dma_start(out=dbg_xa[:], in_=xa[:]),
                            waits=[(cDV, v_xa)], inc=cSY, k=16)
                ev_after = {}
                sil_after = {}
                for ic in range(NI):
                    h = ic % 2
                    if call == 0 and t == 0:
                        emit_recon(6)
                    w_ld = []
                    if call == 0 and t == 0:
                        w_ld.append(recon_store[("g", 2 * ic)])
                        w_ld.append(recon_store[("g", 2 * ic + 1)])
                    if ic >= 2:
                        w_ld.append((cDV, ev_after[ic - 2]))
                    sch.add("sync", lambda e, ic=ic, h=h: e.dma_start(
                        out=gwb[:, h * D:(h + 1) * D], in_=w2g[:, ic * D:(ic + 1) * D]),
                        waits=w_ld, inc=cGW[h], k=16)
                    v_gw = cGW[h].n
                    w_ld2 = []
                    if call == 0 and t == 0:
                        w_ld2.append(recon_store[("u", 2 * ic)])
                        w_ld2.append(recon_store[("u", 2 * ic + 1)])
                    sch.add("sync", lambda e, ic=ic, h=h: e.dma_start(
                        out=uwb[:, h * D:(h + 1) * D], in_=w2u[:, ic * D:(ic + 1) * D]),
                        waits=w_ld2, inc=cUW[h], k=16)
                    v_uw = cUW[h].n
                    w_mm = [(cGW[h], v_gw), (cUW[h], v_uw), (cDV, v_xa)]
                    if ic >= 2:
                        w_mm.append((cDV, ev_after[ic - 2]))
                        w_mm.append((cAC, sil_after[ic - 2]))
                    for k in range(KC):
                        sch.add("tensor", lambda e, k=k, h=h: e.matmul(
                            pg[:, h * T:(h + 1) * T],
                            gwb[:, h * D + k * 128: h * D + (k + 1) * 128],
                            xa[:, k * T:(k + 1) * T], start=(k == 0), stop=(k == KC - 1)),
                            waits=w_mm if k == 0 else ())
                    for k in range(KC):
                        sch.add("tensor", lambda e, k=k, h=h: e.matmul(
                            pu[:, h * T:(h + 1) * T],
                            uwb[:, h * D + k * 128: h * D + (k + 1) * 128],
                            xa[:, k * T:(k + 1) * T], start=(k == 0), stop=(k == KC - 1)),
                            inc=(cPE if k == KC - 1 else None))
                    v_mm = cPE.n
                    sch.add("scalar", lambda e, h=h: e.activation(
                        out=tmp16[:, h * T:(h + 1) * T], in_=pg[:, h * T:(h + 1) * T],
                        func=AF.Silu), waits=[(cPE, v_mm)], inc=cAC)
                    sil_after[ic] = cAC.n
                    sch.add("vector", lambda e, ic=ic, h=h: e.tensor_tensor(
                        big[:, ic * T:(ic + 1) * T], tmp16[:, h * T:(h + 1) * T],
                        pu[:, h * T:(h + 1) * T], ALU.mult),
                        waits=[(cAC, sil_after[ic])], inc=cDV)
                    ev_after[ic] = cDV.n
                v_inter = cDV.n
                if debug_outputs and call == 0 and t == 0:
                    sch.add("sync", lambda e: e.dma_start(out=dbg_int[:], in_=big[:]),
                            waits=[(cDV, v_inter)], inc=cSY, k=16)
                dn_mm_after = {}
                dn_ev = {}
                for hc in range(KC):
                    v_dl = {}

                    def dn_load(q):
                        gi = hc * 4 + q
                        w_ld = []
                        if call == 0 and t == 0:
                            w_ld.append(recon_store[("d", 2 * gi)])
                            w_ld.append(recon_store[("d", 2 * gi + 1)])
                        if dn_mm_after.get(gi - 2):
                            w_ld.append((cPE, dn_mm_after[gi - 2]))
                        sch.add("sync", lambda e, gi=gi, q=q: e.dma_start(
                            out=dwb[:, (q % 2) * D:(q % 2) * D + D],
                            in_=w2d[:, gi * D:(gi + 1) * D]), waits=w_ld, inc=cDW[q % 2], k=16)
                        v_dl[q] = (cDW[q % 2], cDW[q % 2].n)

                    def dn_mm(q):
                        w_mm = [v_dl[q]]
                        if q == 0:
                            w_mm.append((cDV, v_inter))
                            if hc >= 1:
                                w_mm.append((cDV, dn_ev[hc - 1]))
                        for k in range(KC):
                            sch.add("tensor", lambda e, q=q, k=k: e.matmul(
                                pdn, dwb[:, (q % 2) * D + k * 128:(q % 2) * D + (k + 1) * 128],
                                big[:, (q * KC + k) * T:(q * KC + k + 1) * T],
                                start=(q == 0 and k == 0), stop=(q == 3 and k == KC - 1)),
                                waits=w_mm if k == 0 else (),
                                inc=(cPE if k == KC - 1 else None))
                        dn_mm_after[hc * 4 + q] = cPE.n

                    dn_load(0)
                    dn_load(1)
                    dn_mm(0)
                    dn_load(2)
                    dn_mm(1)
                    dn_load(3)
                    dn_mm(2)
                    dn_mm(3)
                    v_dmm = cPE.n
                    if debug_outputs and call == 0 and t == 0 and hc == 0:
                        sch.add("vector", lambda e: e.tensor_copy(
                            tmp32[:, 0:T], pdn), waits=[(cPE, v_dmm)], inc=cDV)
                        sch.add("sync", lambda e: e.dma_start(out=dbg_dn[:], in_=tmp32[:, 0:T]),
                                waits=[(cDV, cDV.n)], inc=cSY, k=16)
                    sch.add("vector", lambda e, hc=hc: e.tensor_tensor(
                        xt[:, hc * T:(hc + 1) * T], pdn, xt[:, hc * T:(hc + 1) * T],
                        ALU.add), waits=[(cPE, v_dmm)], inc=cDV)
                    dn_ev[hc] = cDV.n
                    if call == 0:
                        sch.add("sync", lambda e, hc=hc, t=t: e.dma_start(
                            out=x_fm_ap(hc, t * T, T), in_=xt[:, hc * T:(hc + 1) * T]),
                            waits=[(cDV, dn_ev[hc])], inc=cXST, k=16)
                if call == 0:
                    xt_free_sy = cXST.n
                else:
                    v_done_dv = dn_ev[KC - 1]
                    out_dv = {}
                    ystore = {}
                    for tc in range(4):
                        for g4 in range(4):
                            rnd = tc * 4 + g4
                            pb = pkv[:, T + (rnd % 3) * T: T + (rnd % 3) * T + T]
                            w_tr = [(cDV, v_done_dv)]
                            if rnd >= 3:
                                w_tr.append((cDV, out_dv[rnd - 3]))
                            for j in range(4):
                                kc = g4 * 4 + j
                                sch.add("tensor", lambda e, pb=pb, kc=kc, j=j, tc=tc: e.transpose(
                                    pb[:, j * 128:(j + 1) * 128],
                                    xt[:, kc * T + tc * 128: kc * T + tc * 128 + 128],
                                    ident[:]),
                                    waits=w_tr if j == 0 else (),
                                    inc=(cPE if j == 3 else None))
                            v_tr = cPE.n
                            h2 = rnd % 2
                            w_cp = [(cPE, v_tr)]
                            if prev_ystore:
                                w_cp.append((cSY, prev_ystore))
                            if ystore.get(rnd - 1):
                                w_cp.append((cSY, ystore[rnd - 1]))
                            sch.add("vector", lambda e, pb=pb, h2=h2: e.tensor_copy(
                                tmp32[:, h2 * T:(h2 + 1) * T], pb), waits=w_cp, inc=cDV)
                            out_dv[rnd] = cDV.n
                            sch.add("sync", lambda e, t=t, tc=tc, g4=g4, h2=h2: e.dma_start(
                                out=yout[t * T + tc * 128: t * T + (tc + 1) * 128,
                                         g4 * T:(g4 + 1) * T],
                                in_=tmp32[:, h2 * T:(h2 + 1) * T]),
                                waits=[(cDV, out_dv[rnd])], inc=cSY, k=16)
                            ystore[rnd] = cSY.n
                    prev_ystore = cSY.n
                    xt_free_sy = cSY.n
            return xt_free_sy

        def ph_attnA(xt_free_in):
            qk_tile = big[:, 0:KC * T]
            qkT = big[:, KC * T:2 * KC * T]
            vT = big[:, 2 * KC * T:3 * KC * T]
            st = {"xt_free_pe": None, "last_ac": None, "prev_vT": None,
                  "prev_kv": None, "precast": v_precast}
            for t in range(NTILE):
                for kc in range(KC):
                    w = []
                    if kc == 0:
                        if st["xt_free_pe"]:
                            w.append((cPE, st["xt_free_pe"]))
                            w.append((cAC, st["last_ac"]))
                        if t == 0:
                            w.append((cXST, xt_free_in))
                    sch.add("sync", lambda e, kc=kc, t=t: e.dma_start(
                        out=xt[:, kc * T:(kc + 1) * T], in_=x_fm_ap(kc, t * T, T)),
                        waits=w, inc=cX, k=16)
                chunk_ready = [cX.n] * KC
                extra = [(cAC, st["last_ac"])] if st["last_ac"] else []
                v_bc = ln_stats_xa(chunk_ready, extra)
                v_xa = make_xa(v_bc)

                pstate = {"wb_free": {}, "psum_free": {}, "qk_e": {}, "u_spill": {}}

                def proj(w_src, out_fn, extra_ld_waits=()):
                    res = {}
                    for oc in range(KC):
                        h = oc % 2
                        w_ld = list(extra_ld_waits) if oc == 0 else []
                        if pstate["wb_free"].get(h):
                            w_ld.append((cPE, pstate["wb_free"][h]))
                        sch.add("sync", lambda e, oc=oc, h=h, ws=w_src: e.dma_start(
                            out=awb[:, h * D:(h + 1) * D], in_=ws[:, oc * D:(oc + 1) * D]),
                            waits=w_ld, inc=cAW[h], k=16)
                        v_w = cAW[h].n
                        w_mm = [(cAW[h], v_w), (cDV, v_xa)]
                        if pstate["psum_free"].get(h):
                            w_mm.append(pstate["psum_free"][h])
                        pp = pg if h == 0 else pu
                        for k in range(KC):
                            sch.add("tensor", lambda e, k=k, pp=pp, h=h: e.matmul(
                                pp[:, 0:T], awb[:, h * D + k * 128: h * D + (k + 1) * 128],
                                xa[:, k * T:(k + 1) * T], start=(k == 0),
                                stop=(k == KC - 1)),
                                waits=w_mm if k == 0 else (),
                                inc=(cPE if k == KC - 1 else None))
                        v_mm = cPE.n
                        pstate["wb_free"][h] = v_mm
                        out_fn(oc, pp, h, v_mm, res)
                    return res

                def qk_out(oc, pp, h, v_mm, res):
                    w_z = [(cPE, v_mm)]
                    if pstate["qk_e"].get(h):
                        w_z.append((cAC, pstate["qk_e"][h]))
                    sch.add("vector", lambda e, pp=pp, oc=oc, h=h: e.tensor_scalar(
                        zbuf[:, h * T:(h + 1) * T], pp[:, 0:T], bias_ap(0, oc), 0.0,
                        ALU.add, ALU.add), waits=w_z, inc=cDV)
                    v_z = cDV.n
                    pstate["psum_free"][h] = (cDV, v_z)
                    sch.add("scalar", lambda e, h=h: e.activation(
                        out=tmp16[:, h * T:(h + 1) * T], in_=zbuf[:, h * T:(h + 1) * T],
                        func=AF.Exp), waits=[(cDV, v_z)], inc=cAC)
                    v_e = cAC.n
                    pstate["qk_e"][h] = v_e
                    w_em1 = [(cAC, v_e)]
                    if st["prev_vT"]:
                        w_em1.append((cTR, st["prev_vT"]))
                    sch.add("vector", lambda e, oc=oc, h=h: e.tensor_scalar(
                        qk_tile[:, oc * T:(oc + 1) * T], tmp16[:, h * T:(h + 1) * T],
                        -1.0, 0.0, ALU.add, ALU.add), waits=w_em1, inc=cDV)
                    sch.add("vector", lambda e, h=h: e.tensor_scalar(
                        m8[:, h * T:(h + 1) * T], zbuf[:, h * T:(h + 1) * T], 0.0, 0.0,
                        ALU.is_gt, ALU.bypass), inc=cDV)
                    sch.add("vector", lambda e, oc=oc, h=h: e.copy_predicated(
                        qk_tile[:, oc * T:(oc + 1) * T], m8[:, h * T:(h + 1) * T],
                        zbuf[:, h * T:(h + 1) * T]), inc=cDV)
                    v_q = cDV.n
                    res[oc] = v_q
                    sch.add("scalar", lambda e, oc=oc, t=t: e.dma_start(
                        out=qk_fm_ap(oc, t * T, T), in_=qk_tile[:, oc * T:(oc + 1) * T]),
                        waits=[(cDV, v_q)], inc=cSPL, k=16)

                extra0 = [(cPC, st["precast"])] if st["precast"] else []
                st["precast"] = None
                qk_done = proj(wqk_b, qk_out, extra0)
                v_qk_dv = cDV.n
                for hh in range(HEADS):
                    w_tr = [(cDV, qk_done[hh])]
                    if st["prev_kv"]:
                        w_tr.append((cPE, st["prev_kv"]))
                    sch.add("scalar", lambda e, hh=hh, t=t: e.dma_start_transpose(
                        qkT.rearrange("p (tc f) -> p tc f", tc=4)[:, :, hh * 128:(hh + 1) * 128],
                        qk_tile[:, hh * T:(hh + 1) * T]), waits=w_tr, inc=cTR, k=16)
                v_qkT = cTR.n

                def v_out(oc, pp, h, v_mm, res):
                    w_vs = [(cPE, v_mm), (cTR, v_qkT)]
                    if oc == 0:
                        w_vs.append((cSPL, cSPL.n))
                    sch.add("scalar", lambda e, pp=pp, oc=oc: e.activation(
                        out=qk_tile[:, oc * T:(oc + 1) * T], in_=pp[:, 0:T], func=AF.Silu,
                        bias=bias_ap(1, oc), scale=1.0),
                        waits=w_vs, inc=cAC)
                    v_v = cAC.n
                    pstate["psum_free"][h] = (cAC, v_v)
                    res[oc] = v_v

                v_done = proj(wv_b, v_out)
                for hh in range(HEADS):
                    w_tr = [(cAC, v_done[hh])]
                    if st["prev_kv"]:
                        w_tr.append((cPE, st["prev_kv"]))
                    sch.add("scalar", lambda e, hh=hh: e.dma_start_transpose(
                        vT.rearrange("p (tc f) -> p tc f", tc=4)[:, :, hh * 128:(hh + 1) * 128],
                        qk_tile[:, hh * T:(hh + 1) * T]), waits=w_tr, inc=cTR, k=16)
                v_vT = cTR.n

                def u_out(oc, pp, h, v_mm, res):
                    w_si = [(cPE, v_mm), (cDV, v_qk_dv)]
                    if pstate["u_spill"].get(h):
                        w_si.append((cSPL, cSPL.n))
                    sch.add("scalar", lambda e, pp=pp, oc=oc, h=h: e.activation(
                        out=tmp16[:, h * T:(h + 1) * T], in_=pp[:, 0:T], func=AF.Silu,
                        bias=bias_ap(2, oc), scale=1.0), waits=w_si, inc=cAC)
                    v_u = cAC.n
                    pstate["psum_free"][h] = (cAC, v_u)
                    sch.add("scalar", lambda e, h=h, oc=oc, t=t: e.dma_start(
                        out=u_fm_ap(oc, t * T, T), in_=tmp16[:, h * T:(h + 1) * T]),
                        waits=[(cAC, v_u)], inc=cSPL, k=16)
                    pstate["u_spill"][h] = (cSPL, cSPL.n)
                    res[oc] = v_u

                proj(wu_b, u_out)
                st["last_ac"] = cAC.n

                for tc in range(4):
                    for hh in range(HEADS):
                        first = (t == 0 and tc == 0 and hh % 4 == 0)
                        last = (t == NTILE - 1 and tc == 3)
                        w = [(cTR, v_vT)] if (hh == 0 and tc == 0) else ()
                        sch.add("tensor", lambda e, tc=tc, hh=hh, first=first, last=last: e.matmul(
                            pkv[:, hh * 128:(hh + 1) * 128],
                            qkT[:, tc * D + hh * 128: tc * D + (hh + 1) * 128],
                            vT[:, tc * D + hh * 128: tc * D + (hh + 1) * 128],
                            start=first, stop=last), waits=w,
                            inc=(cPE if (tc == 3 and hh == HEADS - 1) else None))
                st["prev_kv"] = cPE.n
                st["prev_vT"] = v_vT
                st["xt_free_pe"] = st["prev_kv"]
            return st["prev_kv"], st["last_ac"], cTR.n

        def ph_cc(v_kv_mm):
            kvf = xt[:, 0:HEADS * 128]
            sch.add("vector", lambda e, kvf=kvf: e.tensor_copy(kvf, pkv[:]),
                    waits=[(cPE, v_kv_mm)], inc=cDV)
            v_c = cDV.n
            sch.add("gpsimd", lambda e, kvf=kvf: e.dma_start(out=kv_in[:], in_=kvf),
                    waits=[(cDV, v_c)], inc=cKV, k=16)
            v_st = cKV.n
            sch.add("gpsimd", lambda e: e.collective_compute(
                "AllReduce", ALU.add, replica_groups=[list(range(NCORE))],
                ins=[kv_in[:].opt()], outs=[kv_out[:].opt()]),
                waits=[(cKV, v_st)], inc=cCC, k=1)
            v_cc = cCC.n
            sch.add("gpsimd", lambda e: e.dma_start(out=kvsb[:], in_=kv_out[:]),
                    waits=[(cCC, v_cc)], inc=cKV, k=16)
            v_ready = cKV.n
            if debug_outputs:
                sch.add("gpsimd", lambda e: e.dma_start(out=dbg_kv[:], in_=kv_out[:]),
                        waits=[(cKV, v_ready)], inc=cKV, k=16)
                sch.add("gpsimd", lambda e: e.dma_start(out=dbg_kvp[:], in_=kv_in[:]),
                        waits=[(cKV, v_ready)], inc=cKV, k=16)
            return v_ready, v_c

        def ph_attnB(v_kv_ready, xt_free_sy, v_attnA_sc, v_lastacA, o1_prev_pe, v_spills):
            o1 = big[:, 0:KC * T]
            o1sq = big[:, KC * T:2 * KC * T]
            ao = big[:, 2 * KC * T:3 * KC * T]
            st = {"o1mm": {}, "o1ev": {}, "womm": {}, "woev": {}, "aomul": {}, "o2": {}}
            for t in range(NTILE):
                for kc in range(KC):
                    w = []
                    if kc == 0:
                        if xt_free_sy:
                            w.append((cXST, xt_free_sy))
                        if t == 0:
                            w.append((cKV, v_kv_ready))
                            w.append((cPE, o1_prev_pe))
                    sch.add("sync", lambda e, kc=kc, t=t: e.dma_start(
                        out=xt[:, kc * T:(kc + 1) * T], in_=x_fm_ap(kc, t * T, T)),
                        waits=w, inc=cX, k=16)
                v_xtile = cX.n
                xt_free_sy = None
                for hh in range(HEADS):
                    h = hh % 2
                    w_ld = []
                    if st["o1mm"].get(hh - 2):
                        w_ld.append((cPE, st["o1mm"][hh - 2]))
                    if t == 0 and hh < 2:
                        w_ld.append((cSPL, v_spills))
                    sch.add("sync", lambda e, hh=hh, h=h, t=t: e.dma_start(
                        out=qch[:, h * T:(h + 1) * T], in_=qk_fm_ap(hh, t * T, T)),
                        waits=w_ld, inc=cQC[h], k=16)
                    v_q = cQC[h].n
                    pp = pg if h == 0 else pu
                    w_mm = [(cQC[h], v_q)]
                    if t == 0 and hh == 0:
                        w_mm.append((cGP, v_kv_ready))
                        w_mm.append((cAC, v_lastacA))
                    if st["o1ev"].get(hh - 2):
                        w_mm.append((cAC, st["o1ev"][hh - 2]))
                    if t > 0 and hh < 2 and st["womm"].get(KC - 2 + hh):
                        w_mm.append((cDV, st["woev"][KC - 2 + hh]))
                    sch.add("tensor", lambda e, hh=hh, h=h, pp=pp: e.matmul(
                        pp[:, 0:T], kvsb[:, hh * 128:(hh + 1) * 128],
                        qch[:, h * T:(h + 1) * T], start=True, stop=True),
                        waits=w_mm, inc=cPE)
                    st["o1mm"][hh] = cPE.n
                    v_mm = cPE.n
                    w_cp = [(cPE, v_mm)]
                    if t == 0 and hh == 0:
                        w_cp.append((cTR, v_attnA_sc))
                    sch.add("scalar", lambda e, hh=hh, pp=pp: e.copy(
                        o1[:, hh * T:(hh + 1) * T], pp[:, 0:T]),
                        waits=w_cp, inc=cAC)
                    sch.add("scalar", lambda e, hh=hh: e.activation(
                        out=o1sq[:, hh * T:(hh + 1) * T], in_=o1[:, hh * T:(hh + 1) * T],
                        func=AF.Square), inc=cAC)
                    st["o1ev"][hh] = cAC.n
                    v_ev = cAC.n
                    sch.add("tensor", lambda e, hh=hh: e.matmul(
                        pdn[0:1, :], onesc[:], o1[:, hh * T:(hh + 1) * T],
                        start=(hh == 0), stop=(hh == HEADS - 1)), waits=[(cAC, v_ev)])
                    sch.add("tensor", lambda e, hh=hh: e.matmul(
                        psq[0:1, :], onesc[:], o1sq[:, hh * T:(hh + 1) * T],
                        start=(hh == 0), stop=(hh == HEADS - 1)),
                        inc=(cPE if hh == HEADS - 1 else None))
                v_sums = cPE.n
                v_bc = _row_math(pdn[0:1, :], psq[0:1, :], v_sums, cPE)
                for hh in range(HEADS):
                    h = hh % 2
                    w_u = []
                    if st["aomul"].get(hh - 2):
                        w_u.append((cDV, st["aomul"][hh - 2]))
                    if t == 0 and hh < 2:
                        w_u.append((cSPL, v_spills))
                    sch.add("sync", lambda e, hh=hh, h=h, t=t: e.dma_start(
                        out=uch[:, h * T:(h + 1) * T], in_=u_fm_ap(hh, t * T, T)),
                        waits=w_u, inc=cUC[h], k=16)
                    v_u = cUC[h].n
                    w_ap = [(cDV, v_bc)] if hh == 0 else []
                    if st["o2"].get(hh - 2):
                        w_ap.append((cAC, st["o2"][hh - 2]))
                    sch.add("vector", lambda e, hh=hh, h=h: e.tensor_tensor(
                        tmp32[:, h * T:(h + 1) * T], o1[:, hh * T:(hh + 1) * T], bcA[:],
                        ALU.mult), waits=w_ap)
                    sch.add("vector", lambda e, h=h: e.tensor_tensor(
                        tmp32[:, h * T:(h + 1) * T], tmp32[:, h * T:(h + 1) * T], bcB[:],
                        ALU.subtract), inc=cDV)
                    v_n = cDV.n
                    sch.add("scalar", lambda e, hh=hh, h=h: e.activation(
                        out=tmp16[:, h * T:(h + 1) * T], in_=tmp32[:, h * T:(h + 1) * T],
                        func=AF.Identity, scale=bias_ap(4, hh), bias=bias_ap(5, hh)),
                        waits=[(cDV, v_n)], inc=cAC)
                    v_o2 = cAC.n
                    st["o2"][hh] = v_o2
                    sch.add("vector", lambda e, hh=hh, h=h: e.tensor_tensor(
                        ao[:, hh * T:(hh + 1) * T], tmp16[:, h * T:(h + 1) * T],
                        uch[:, h * T:(h + 1) * T], ALU.mult),
                        waits=[(cAC, v_o2), (cUC[h], v_u)], inc=cDV)
                    st["aomul"][hh] = cDV.n
                v_ao = cDV.n
                for oc in range(KC):
                    h = oc % 2
                    w_ld = []
                    if st["womm"].get(oc - 2):
                        w_ld.append((cPE, st["womm"][oc - 2]))
                    sch.add("sync", lambda e, oc=oc, h=h: e.dma_start(
                        out=awb[:, h * D:(h + 1) * D], in_=wo_b[:, oc * D:(oc + 1) * D]),
                        waits=w_ld, inc=cAW[h], k=16)
                    v_w = cAW[h].n
                    pp = pg if h == 0 else pu
                    w_mm = [(cAW[h], v_w), (cDV, v_ao)]
                    if st["woev"].get(oc - 2):
                        w_mm.append((cDV, st["woev"][oc - 2]))
                    for k in range(KC):
                        sch.add("tensor", lambda e, k=k, h=h, pp=pp: e.matmul(
                            pp[:, 0:T], awb[:, h * D + k * 128: h * D + (k + 1) * 128],
                            ao[:, k * T:(k + 1) * T], start=(k == 0), stop=(k == KC - 1)),
                            waits=w_mm if k == 0 else (),
                            inc=(cPE if k == KC - 1 else None))
                    st["womm"][oc] = cPE.n
                    v_mm = cPE.n
                    sch.add("scalar", lambda e, oc=oc, h=h, pp=pp: e.activation(
                        out=tmp32[:, h * T:(h + 1) * T], in_=pp[:, 0:T], func=AF.Identity,
                        scale=1.0, bias=bias_ap(3, oc)), waits=[(cPE, v_mm)], inc=cAC)
                    v_b = cAC.n
                    sch.add("vector", lambda e, oc=oc, h=h: e.tensor_tensor(
                        xt[:, oc * T:(oc + 1) * T], tmp32[:, h * T:(h + 1) * T],
                        xt[:, oc * T:(oc + 1) * T], ALU.add),
                        waits=[(cAC, v_b), (cX, v_xtile)], inc=cDV)
                    st["woev"][oc] = cDV.n
                    sch.add("sync", lambda e, oc=oc, t=t: e.dma_start(
                        out=x_fm_ap(oc, t * T, T), in_=xt[:, oc * T:(oc + 1) * T]),
                        waits=[(cDV, st["woev"][oc])], inc=cXST, k=16)
                xt_free_sy = cXST.n
            return xt_free_sy

        xt_free = ph_ffn(0, v_pre_xst)
        if debug_outputs:
            sch.add("sync", lambda e: e.dma_start(out=dbg1[:], in_=x_fm[:]),
                    waits=[(cXST, xt_free)], inc=cXST, k=16)
            xt_free = cXST.n
        v_kv, v_lastac, v_attnA_sc = ph_attnA(xt_free)
        v_kv_ready, _ = ph_cc(v_kv)
        xt_free = ph_attnB(v_kv_ready, xt_free, v_attnA_sc, v_lastac, v_kv, cSPL.n)
        if debug_outputs:
            sch.add("sync", lambda e: e.dma_start(out=dbg2[:], in_=x_fm[:]),
                    waits=[(cXST, xt_free)], inc=cXST, k=16)
            xt_free = cXST.n
        ph_ffn(1, xt_free)

        _all_ctrs = [cSY, cSC, cGP, cPE, cDV, cAC, cCC, cX, cXST, cSTG, cMSK,
                     cPC, cSPL, cTR, cKV] + cGW + cUW + cDW + cAW + cQC + cUC + cW2
        for c in _all_ctrs:
            if c.n > 0:
                sch.add("sync", lambda e: e.nop(), waits=[(c, c.n)])

        @block.tensor
        def _(e):
            sch.replay("tensor", e, sems)

        @block.vector
        def _(e):
            sch.replay("vector", e, sems)

        @block.scalar
        def _(e):
            sch.replay("scalar", e, sems)

        @block.sync
        def _(e):
            sch.replay("sync", e, sems)

        @block.gpsimd
        def _(e):
            sch.replay("gpsimd", e, sems)

    return nc


def _prep_inputs(inputs):
    tb = _tables()
    x = np.ascontiguousarray(np.asarray(inputs["L0_attn_x"], np.float32).reshape(NTOK, D))
    bias_m = np.zeros((128, 6 * 16), np.float32)
    for i, name in enumerate(["L0_attn_bqk", "L0_attn_bv", "L0_attn_bu", "L0_attn_bo",
                              "L0_attn_ln_g", "L0_attn_ln_b"]):
        bias_m[:, i * 16:(i + 1) * 16] = np.asarray(inputs[name], np.float32).reshape(16, 128).T
    basis_m = np.zeros((32, 512), np.float32)
    basis_m[0:20, 0:128] = tb["UA32"]
    basis_m[0:20, 128:256] = tb["UB32"]
    basis_m[0:20, 256:384] = 0.5 * tb["UA32"]
    basis_m[0:20, 384:512] = 0.5 * tb["UB32"]
    shared = dict(
        wqk=np.ascontiguousarray(np.asarray(inputs["L0_attn_Wqk"], np.float32)),
        wv=np.ascontiguousarray(np.asarray(inputs["L0_attn_Wv"], np.float32)),
        wu=np.ascontiguousarray(np.asarray(inputs["L0_attn_Wu"], np.float32)),
        wo=np.ascontiguousarray(np.asarray(inputs["L0_attn_Wo"], np.float32)),
        biases=bias_m,
        cptg=_cpt(np.asarray(inputs["L0_ffn_gate_cp"], np.float32), tb["b_of_gu"]),
        cptu=_cpt(np.asarray(inputs["L0_ffn_up_cp"], np.float32), tb["b_of_gu"]),
        cptd=_cpt(np.asarray(inputs["L0_ffn_down_cp"], np.float32), tb["b_of_dn"]),
        phig=tb["PHI_gu"], phid=tb["PHI_dn"],
        mskg=tb["MSK_gu"], mskd=tb["MSK_dn"],
        basis=basis_m,
        onesr_d=np.ones((1, 128), np.float32),
    )
    in_maps = []
    for c in range(NCORE):
        m = dict(shared)
        m["x"] = np.ascontiguousarray(x[c * PTOK:(c + 1) * PTOK])
        in_maps.append(m)
    return in_maps


def kernel(**inputs):
    in_maps = _prep_inputs(inputs)
    nc = _build()
    res = run_bass_kernel_spmd(nc, in_maps, core_ids=list(range(NCORE)))
    out = np.concatenate([np.asarray(res.results[c]["y"]) for c in range(NCORE)], axis=0)
    return out.reshape(1, NTOK, D).astype(np.float32)


# version 05 (final)
# speedup vs baseline: 1.0222x; 1.0222x over previous
import numpy as np
from math import comb

import concourse.bass as bass
import concourse.mybir as mybir
from concourse.bass_utils import run_bass_kernel_spmd

F32 = mybir.dt.float32
F32R = mybir.dt.float32r
BF16 = mybir.dt.bfloat16
I8 = mybir.dt.int8
I32 = mybir.dt.int32
AF = mybir.ActivationFunctionType
ALU = mybir.AluOpType

D = 2048
NTOK = 16384
NCORE = 8
PTOK = NTOK // NCORE
T = 512
NTILE = PTOK // T
KC = D // 128
HEADS = 16
INTER = 8192
NI = INTER // 128
NCTRL = 130156
NS = 16777216
NBLK = NS // 128
LN_EPS = 1e-5

ENGINES = ("tensor", "vector", "scalar", "sync", "gpsimd")

_POLY = {
    0: np.array([0.0, -0.5, 1.0, -0.5]),
    1: np.array([1.0, 0.0, -2.5, 1.5]),
    2: np.array([0.0, 0.5, 2.0, -1.5]),
    3: np.array([0.0, 0.0, -0.5, 0.5]),
}
_CACHE = {}


def _tables():
    if "t" in _CACHE:
        return _CACHE["t"]
    t = np.linspace(0.0, NCTRL - 1.0, NS, dtype=np.float64)
    i_all = np.clip(np.floor(t).astype(np.int64), 0, NCTRL - 2)
    step = (NCTRL - 1.0) / (NS - 1.0)
    iA = i_all[0::128].copy()
    phi = t[0::128] - iA
    i_blk = i_all.reshape(NBLK, 128)
    crossed = i_blk > iA[:, None]
    sc = np.where(crossed.any(axis=1), crossed.argmax(axis=1), 128).astype(np.int64)

    s = np.arange(128, dtype=np.float64)
    sdt = s * step
    UA = np.zeros((20, 128))
    UB = np.zeros((20, 128))
    for l in range(4):
        for ti, tau in enumerate((-1, 0, 1, 2, 3)):
            row = 4 * ti + l
            if -1 <= tau <= 2:
                pa = _POLY[tau + 1]
                UA[row] = sum(pa[j] * comb(j, l) * sdt ** (j - l) for j in range(l, 4))
            if 0 <= tau <= 3:
                pb = _POLY[tau]
                UB[row] = sum(pb[j] * comb(j, l) * (sdt - 1.0) ** (j - l) for j in range(l, 4))

    bp = np.arange(NBLK, dtype=np.int64)
    b_of_gu = 16 * (128 * (bp // 2048) + (bp % 128)) + (bp % 2048) // 128
    b_of_dn = 64 * (128 * (bp // 8192) + (bp % 128)) + (bp % 8192) // 128

    phi_pows = np.stack([phi ** l for l in range(4)], axis=0)
    PHI20 = np.concatenate([phi_pows] * 5, axis=0)
    s_idx = np.arange(128)[:, None]
    msk = (s_idx >= sc[None, :]).astype(np.int8)

    res = dict(
        iA=iA,
        UA32=UA.astype(np.float32), UB32=UB.astype(np.float32),
        b_of_gu=b_of_gu, b_of_dn=b_of_dn,
        PHI_gu=np.ascontiguousarray(PHI20[:, b_of_gu]).astype(np.float32),
        PHI_dn=np.ascontiguousarray(PHI20[:, b_of_dn]).astype(np.float32),
        MSK_gu=np.ascontiguousarray(msk[:, b_of_gu]),
        MSK_dn=np.ascontiguousarray(msk[:, b_of_dn]),
    )
    _CACHE["t"] = res
    return res


def _cpt(cp, perm_b_of):
    tb = _tables()
    iA = tb["iA"][perm_b_of]
    rows = []
    for tau in (-1, 0, 1, 2, 3):
        v = np.take(cp, np.clip(iA + tau, 0, NCTRL - 1))
        rows.extend([v, v, v, v])
    return np.ascontiguousarray(np.stack(rows, axis=0).astype(np.float32))


class Ctr:
    __slots__ = ("name", "n")

    def __init__(self, name):
        self.name = name
        self.n = 0


class Sched:
    def __init__(self):
        self.ops = {e: [] for e in ENGINES}

    def add(self, eng, fn, waits=(), inc=None, k=1):
        after = None
        if inc is not None:
            inc.n += k
            after = (inc, k)
        ws = tuple((c, v) for (c, v) in waits if v and v > 0)
        self.ops[eng].append((ws, fn, after))
        return inc.n if inc is not None else None

    def replay(self, eng_name, eng, sems):
        for ws, fn, after in self.ops[eng_name]:
            for c, v in ws:
                eng.wait_ge(sems[c.name], v)
            instr = fn(eng)
            if after is not None:
                c, k = after
                assert instr is not None
                instr.then_inc(sems[c.name], k)


def _build(debug_outputs=False):
    nc = bass.Bass("TRN2", target_bir_lowering=False, debug=False)

    xin = nc.dram_tensor("x", [PTOK, D], F32, kind="ExternalInput").ap()
    wqk = nc.dram_tensor("wqk", [D, D], F32, kind="ExternalInput").ap()
    wv = nc.dram_tensor("wv", [D, D], F32, kind="ExternalInput").ap()
    wu = nc.dram_tensor("wu", [D, D], F32, kind="ExternalInput").ap()
    wo = nc.dram_tensor("wo", [D, D], F32, kind="ExternalInput").ap()
    biases = nc.dram_tensor("biases", [128, 6 * 16], F32, kind="ExternalInput").ap()
    cptg = nc.dram_tensor("cptg", [20, NBLK], F32, kind="ExternalInput").ap()
    cptu = nc.dram_tensor("cptu", [20, NBLK], F32, kind="ExternalInput").ap()
    cptd = nc.dram_tensor("cptd", [20, NBLK], F32, kind="ExternalInput").ap()
    phig = nc.dram_tensor("phig", [20, NBLK], F32, kind="ExternalInput").ap()
    phid = nc.dram_tensor("phid", [20, NBLK], F32, kind="ExternalInput").ap()
    mskg = nc.dram_tensor("mskg", [128, NBLK], I8, kind="ExternalInput").ap()
    mskd = nc.dram_tensor("mskd", [128, NBLK], I8, kind="ExternalInput").ap()
    basis = nc.dram_tensor("basis", [32, 512], F32R, kind="ExternalInput").ap()
    onesr_d = nc.dram_tensor("onesr_d", [1, 128], F32R, kind="ExternalInput").ap()

    yout = nc.dram_tensor("y", [PTOK, D], F32, kind="ExternalOutput").ap()
    if debug_outputs:
        w2g = nc.dram_tensor("w2g", [128, NBLK], BF16, kind="ExternalOutput").ap()
        w2u = nc.dram_tensor("w2u", [128, NBLK], BF16, kind="ExternalOutput").ap()
        w2d = nc.dram_tensor("w2d", [128, NBLK], BF16, kind="ExternalOutput").ap()
    else:
        w2g = nc.dram_tensor("w2g", [128, NBLK], BF16, kind="Internal").ap()
        w2u = nc.dram_tensor("w2u", [128, NBLK], BF16, kind="Internal").ap()
        w2d = nc.dram_tensor("w2d", [128, NBLK], BF16, kind="Internal").ap()

    if debug_outputs:
        dbg1 = nc.dram_tensor("dbg1", [128, KC * PTOK], F32, kind="ExternalOutput").ap()
        dbg2 = nc.dram_tensor("dbg2", [128, KC * PTOK], F32, kind="ExternalOutput").ap()
        dbg_xa = nc.dram_tensor("dbg_xa", [128, KC * T], BF16, kind="ExternalOutput").ap()
        dbg_dn = nc.dram_tensor("dbg_dn", [128, T], F32, kind="ExternalOutput").ap()
        dbg_int = nc.dram_tensor("dbg_int", [128, NI * T], BF16, kind="ExternalOutput").ap()
    x_fm = nc.dram_tensor("x_fm", [128, KC * PTOK], F32, kind="Internal").ap()
    wqk_b = nc.dram_tensor("wqk_b", [128, KC * D], BF16, kind="Internal").ap()
    wv_b = nc.dram_tensor("wv_b", [128, KC * D], BF16, kind="Internal").ap()
    wu_b = nc.dram_tensor("wu_b", [128, KC * D], BF16, kind="Internal").ap()
    wo_b = nc.dram_tensor("wo_b", [128, KC * D], BF16, kind="Internal").ap()
    _dk = "ExternalOutput" if debug_outputs else "Internal"
    qk_fm = nc.dram_tensor("qk_fm", [128, KC * PTOK], BF16, kind=_dk).ap()
    u_fm = nc.dram_tensor("u_fm", [128, KC * PTOK], BF16, kind=_dk).ap()
    kv_in = nc.dram_tensor("kv_in", [128, HEADS * 128], F32, kind="Internal").ap()
    kv_out = nc.dram_tensor("kv_out", [128, HEADS * 128], F32, kind="Internal").ap()
    if debug_outputs:
        dbg_kv = nc.dram_tensor("dbg_kv", [128, HEADS * 128], F32, kind="ExternalOutput").ap()
        dbg_kvp = nc.dram_tensor("dbg_kvp", [128, HEADS * 128], F32, kind="ExternalOutput").ap()

    sch = Sched()
    cSY = Ctr("sy")
    cSC = Ctr("sc")
    cGP = Ctr("gp")
    cPE = Ctr("pe")
    cDV = Ctr("dv")
    cAC = Ctr("ac")
    cCC = Ctr("cc")
    cX = Ctr("x")
    cXST = Ctr("xst")
    cGW = [Ctr("gw0"), Ctr("gw1")]
    cUW = [Ctr("uw0"), Ctr("uw1")]
    cDW = [Ctr("dw0"), Ctr("dw1")]
    cAW = [Ctr("aw0"), Ctr("aw1")]
    cQC = [Ctr("qc0"), Ctr("qc1")]
    cUC = [Ctr("uc0"), Ctr("uc1")]
    cSTG = Ctr("stg")
    cMSK = Ctr("msk")
    cW2 = [Ctr("w20"), Ctr("w21")]
    cPC = Ctr("pc")
    cSPL = Ctr("spl")
    cTR = Ctr("tr")
    cKV = Ctr("kv")

    def x_fm_ap(kc, t0, n):
        return x_fm[:].rearrange("p (kc t) -> p kc t", kc=KC)[:, kc, t0:t0 + n]

    def qk_fm_ap(kc, t0, n):
        return qk_fm[:].rearrange("p (kc t) -> p kc t", kc=KC)[:, kc, t0:t0 + n]

    def u_fm_ap(kc, t0, n):
        return u_fm[:].rearrange("p (kc t) -> p kc t", kc=KC)[:, kc, t0:t0 + n]

    import contextlib
    with contextlib.ExitStack() as _stack:
        xt = _stack.enter_context(nc.sbuf_tensor("xt", [128, KC * T], F32))
        xa = _stack.enter_context(nc.sbuf_tensor("xa", [128, KC * T], BF16))
        big = _stack.enter_context(nc.sbuf_tensor("big", [128, NI * T], BF16))
        xbc = _stack.enter_context(nc.sbuf_tensor("xbc", [128, 2 * T], BF16))
        xsqc = _stack.enter_context(nc.sbuf_tensor("xsqc", [128, 2 * T], BF16))
        srowA = _stack.enter_context(nc.sbuf_tensor("srowA", [65, T], F32))
        srowB = _stack.enter_context(nc.sbuf_tensor("srowB", [65, T], F32))
        srr_t = _stack.enter_context(nc.sbuf_tensor("srr", [1, T], F32R))
        bcA = _stack.enter_context(nc.sbuf_tensor("bcA", [128, T], F32))
        bcB = _stack.enter_context(nc.sbuf_tensor("bcB", [128, T], F32))
        gwb = _stack.enter_context(nc.sbuf_tensor("gwb", [128, 2 * D], BF16))
        uwb = _stack.enter_context(nc.sbuf_tensor("uwb", [128, 2 * D], BF16))
        dwb = _stack.enter_context(nc.sbuf_tensor("dwb", [128, 2 * D], BF16))
        awb = _stack.enter_context(nc.sbuf_tensor("awb", [128, 2 * D], BF16))
        tmp16 = _stack.enter_context(nc.sbuf_tensor("tmp16", [128, 2 * T], BF16))
        tmp32 = _stack.enter_context(nc.sbuf_tensor("tmp32", [128, 2 * T], F32))
        zbuf = _stack.enter_context(nc.sbuf_tensor("zbuf", [128, 2 * T], F32))
        m8 = _stack.enter_context(nc.sbuf_tensor("m8", [128, 2 * T], I8))
        stg_c = _stack.enter_context(nc.sbuf_tensor("stg_c", [20, 1024], F32))
        stg_p = _stack.enter_context(nc.sbuf_tensor("stg_p", [20, 1024], F32))
        fsb = _stack.enter_context(nc.sbuf_tensor("fsb", [32, 1024], F32R))
        mskb = _stack.enter_context(nc.sbuf_tensor("mskb", [128, 1024], I8))
        wsb = _stack.enter_context(nc.sbuf_tensor("wsb", [128, 2 * 1024], BF16))
        bas = _stack.enter_context(nc.sbuf_tensor("bas", [32, 512], F32R))
        bias_sb = _stack.enter_context(nc.sbuf_tensor("bias_sb", [128, 6 * 16], F32))
        onesc = _stack.enter_context(nc.sbuf_tensor("onesc", [128, 1], BF16))
        onesr = _stack.enter_context(nc.sbuf_tensor("onesr", [1, 128], F32R))
        ident = _stack.enter_context(nc.sbuf_tensor("ident", [128, 128], F32))
        io1 = _stack.enter_context(nc.sbuf_tensor("io1", [128, 128], I32))
        io2 = _stack.enter_context(nc.sbuf_tensor("io2", [128, 128], I32))
        kvsb = _stack.enter_context(nc.sbuf_tensor("kvsb", [128, HEADS * 128], BF16))
        qch = _stack.enter_context(nc.sbuf_tensor("qch", [128, 2 * T], BF16))
        uch = _stack.enter_context(nc.sbuf_tensor("uch", [128, 2 * T], BF16))
        pg = _stack.enter_context(nc.psum_tensor("pg", [128, 2 * T], F32))
        pu = _stack.enter_context(nc.psum_tensor("pu", [128, 2 * T], F32))
        pkv = _stack.enter_context(nc.psum_tensor("pkv", [128, 4 * T], F32))
        _sem_names = ["sy", "sc", "gp", "pe", "dv", "ac", "cc", "x", "xst",
                      "gw0", "gw1", "uw0", "uw1", "dw0", "dw1", "aw0", "aw1",
                      "qc0", "qc1", "uc0", "uc1", "stg", "msk", "w20", "w21",
                      "pc", "spl", "tr", "kv"]
        sems = {n: _stack.enter_context(nc.semaphore("s_" + n)) for n in _sem_names}
        block = _stack.enter_context(nc.Block())

        pdn = pkv[:, 0:T]
        psq = pkv[:, T:2 * T]
        p_recA = pkv[:, T:2 * T]
        p_recB = pkv[:, 2 * T:3 * T]

        def bias_ap(idx, kc):
            return bias_sb[:, idx * 16 + kc: idx * 16 + kc + 1]

        sch.add("gpsimd", lambda e: e.dma_start(out=bias_sb[:], in_=biases[:]), inc=cGP, k=16)
        sch.add("gpsimd", lambda e: e.dma_start(out=bas[:], in_=basis[:]), inc=cGP, k=16)
        sch.add("gpsimd", lambda e: e.dma_start(out=onesr[:], in_=onesr_d[:]), inc=cGP, k=16)
        v_smalls = cGP.n
        sch.add("gpsimd", lambda e: e.memset(onesc[:], 1.0), inc=cGP, k=1)
        v_onesc = cGP.n
        sch.add("gpsimd", lambda e: e.iota(io1[:], pattern=[[1, 128]], base=0, channel_multiplier=0), inc=cGP, k=1)
        sch.add("gpsimd", lambda e: e.iota(io2[:], pattern=[[0, 128]], base=0, channel_multiplier=1), inc=cGP, k=1)
        v_io = cGP.n
        sch.add("vector", lambda e: e.tensor_tensor(ident[:], io1[:], io2[:], ALU.is_equal),
                waits=[(cGP, v_io)], inc=cDV)
        ident_done = cDV.n
        def emit_precasts():
            for (wsrc, wdst) in ((wqk, wqk_b), (wv, wv_b), (wu, wu_b), (wo, wo_b)):
                for oc in range(KC):
                    src_r = wsrc.rearrange("(k p) (oc o2) -> p oc k o2", p=128, o2=128)[:, oc]
                    dst_r = wdst[:].rearrange("p (oc k o2) -> p oc k o2", k=KC, o2=128)[:, oc]
                    sch.add("gpsimd", lambda e, s=src_r, d=dst_r: e.dma_start(out=d, in_=s),
                            inc=cPC, k=16)
            return cPC.n

        pe_done_tc = {}
        dv_done_round = {}
        sy_store_round = {}
        n_tc = PTOK // 128

        def prelude_tc(tc):
                half = tc % 2
                stage = xt[:, half * 2048: half * 2048 + 2048]
                w = [(cPE, pe_done_tc.get(tc - 2))] if tc >= 2 else []
                sch.add("sync", lambda e, st=stage, tc=tc: e.dma_start(
                    out=st, in_=xin[tc * 128:(tc + 1) * 128, :]), waits=w, inc=cSY, k=16)
                v_ld = cSY.n
                for g4 in range(4):
                    rnd = tc * 4 + g4
                    pb = pkv[:, 0:T] if rnd % 2 == 0 else pkv[:, 3 * T:4 * T]
                    wts = [(cSY, v_ld), (cDV, ident_done)]
                    if rnd >= 2:
                        wts.append((cDV, dv_done_round[rnd - 2]))
                    for j in range(4):
                        kc = g4 * 4 + j
                        sch.add("tensor",
                                lambda e, pb=pb, st=stage, kc=kc, j=j: e.transpose(
                                    pb[:, j * 128:(j + 1) * 128],
                                    st[:, kc * 128:(kc + 1) * 128], ident[:]),
                                waits=wts if j == 0 else (),
                                inc=(cPE if j == 3 else None))
                    v_pe = cPE.n
                    if g4 == 3:
                        pe_done_tc[tc] = v_pe
                    h2 = rnd % 2
                    w_cp = [(cPE, v_pe)]
                    if sy_store_round.get(rnd - 1):
                        w_cp.append((cXST, sy_store_round[rnd - 1]))
                    sch.add("vector", lambda e, pb=pb, h2=h2: e.tensor_copy(
                        tmp32[:, h2 * T:(h2 + 1) * T], pb), waits=w_cp, inc=cDV)
                    dv_done_round[rnd] = cDV.n
                    sch.add("sync", lambda e, tc=tc, g4=g4, h2=h2: e.dma_start(
                        out=x_fm[:].rearrange("p (kc t) -> p kc t", kc=KC)[
                            :, g4 * 4:(g4 + 1) * 4, tc * 128:(tc + 1) * 128],
                        in_=tmp32[:, h2 * T:(h2 + 1) * T].rearrange("p (j t) -> p j t", j=4)),
                        waits=[(cDV, dv_done_round[rnd])], inc=cXST, k=16)
                    sy_store_round[rnd] = cXST.n

        recon_store = {}
        rstate = {"parity": 0, "half_user": {}, "tiles": 0, "last_cp": 0, "last_pred": 0}

        def recon_chunk(wname, cpt_d, phi_d, msk_d, w2_d, basA, basB, ch):
            half = rstate["parity"]
            rstate["parity"] ^= 1
            fsb_v = fsb[:, 0:1024]
            msk_v = mskb[:, 0:1024]
            wsb_v = wsb[:, half * 1024: half * 1024 + 1024]
            sl = slice(ch * 1024, (ch + 1) * 1024)
            prev = rstate["half_user"].get(half)
            last = rstate.get("lastchunk")
            w_ld = [(cDV, last["fbuild"])] if last else []
            sch.add("gpsimd", lambda e, s=stg_c, d=cpt_d, sl=sl: e.dma_start(
                out=s[:], in_=d[:, sl]), waits=w_ld, inc=cSTG, k=16)
            sch.add("gpsimd", lambda e, s=stg_p, d=phi_d, sl=sl: e.dma_start(
                out=s[:], in_=d[:, sl]), inc=cSTG, k=16)
            v_stg = cSTG.n
            w_msk = [(cDV, last["lastpred"])] if last else []
            sch.add("scalar", lambda e, m=msk_v, d=msk_d, sl=sl: e.dma_start(
                out=m, in_=d[:, sl]), waits=w_msk, inc=cMSK, k=16)
            v_msk = cMSK.n
            w_fb = [(cSTG, v_stg)]
            if last:
                w_fb.append((cPE, last["mm"]))
            sch.add("vector", lambda e: e.tensor_tensor(
                fsb_v[0:20, :], stg_c[:], stg_p[:], ALU.mult), waits=w_fb, inc=cDV)
            v_fb = cDV.n
            mm_last = None
            for q in range(2):
                tl = slice(q * 512, (q + 1) * 512)
                w_mm = [(cDV, v_fb), (cGP, v_smalls)]
                if rstate["tiles"] > 0:
                    w_mm.append((cAC, rstate["last_cp"]))
                    w_mm.append((cDV, rstate["last_pred"]))
                sch.add("tensor", lambda e, f=fsb_v, tl=tl, bA=basA: e.matmul(
                    p_recA, bA, f[0:20, tl], start=True, stop=True), waits=w_mm)
                sch.add("tensor", lambda e, f=fsb_v, tl=tl, bB=basB: e.matmul(
                    p_recB, bB, f[0:20, tl], start=True, stop=True), inc=cPE)
                v_mm = cPE.n
                mm_last = v_mm
                w_cp = [(cPE, v_mm)]
                if q == 0 and prev:
                    w_cp.append((prev["store_ctr"], prev["store"]))
                sch.add("scalar", lambda e, w=wsb_v, tl=tl: e.copy(w[:, tl], p_recA),
                        waits=w_cp, inc=cAC)
                rstate["last_cp"] = cAC.n
                sch.add("vector", lambda e, w=wsb_v, m=msk_v, tl=tl: e.copy_predicated(
                    w[:, tl], m[:, tl], p_recB),
                    waits=[(cAC, rstate["last_cp"]), (cMSK, v_msk)], inc=cDV)
                rstate["last_pred"] = cDV.n
                rstate["tiles"] += 1
            cw2 = cW2[half]
            w_st = [(cDV, rstate["last_pred"])]
            if cw2.n > 0:
                w_st.append((cw2, cw2.n))
            sch.add("scalar", lambda e, w=wsb_v, d=w2_d, sl=sl: e.dma_start(
                out=d[:, sl], in_=w), waits=w_st, inc=cw2, k=16)
            recon_store[(wname, ch)] = (cw2, cw2.n)
            rec = dict(store_ctr=cw2, store=cw2.n, fbuild=v_fb, mm=mm_last,
                       lastpred=rstate["last_pred"])
            rstate["half_user"][half] = rec
            rstate["lastchunk"] = rec

        basA_gu = bas[0:20, 0:128]
        basB_gu = bas[0:20, 128:256]
        basA_d = bas[0:20, 256:384]
        basB_d = bas[0:20, 384:512]
        recon_jobs = []
        for ic in range(NI):
            recon_jobs.append(("g", cptg, phig, mskg, w2g, basA_gu, basB_gu, 2 * ic))
            recon_jobs.append(("g", cptg, phig, mskg, w2g, basA_gu, basB_gu, 2 * ic + 1))
            recon_jobs.append(("u", cptu, phig, mskg, w2u, basA_gu, basB_gu, 2 * ic))
            recon_jobs.append(("u", cptu, phig, mskg, w2u, basA_gu, basB_gu, 2 * ic + 1))
        for ch in range(2 * NI):
            recon_jobs.append(("d", cptd, phid, mskd, w2d, basA_d, basB_d, ch))
        _rq = {"i": 0}

        def emit_recon(n):
            while n > 0 and _rq["i"] < len(recon_jobs):
                recon_chunk(*recon_jobs[_rq["i"]])
                _rq["i"] += 1
                n -= 1

        for tc in range(n_tc):
            prelude_tc(tc)
            emit_recon(4)
        v_precast = emit_precasts()
        v_pre_xst = cXST.n

        stats_mm_after = {}

        def ln_stats_xa(chunk_ready, extra_first_mm_waits=()):
            for kc in range(KC):
                h = kc % 2
                w = [(cX, chunk_ready[kc])]
                if kc >= 2:
                    w.append((cPE, stats_mm_after[kc - 2]))
                sch.add("scalar", lambda e, kc=kc, h=h: e.copy(
                    xbc[:, h * T:(h + 1) * T], xt[:, kc * T:(kc + 1) * T]), waits=w, inc=cAC)
                sch.add("scalar", lambda e, h=h: e.activation(
                    out=xsqc[:, h * T:(h + 1) * T], in_=xbc[:, h * T:(h + 1) * T],
                    func=AF.Square), inc=cAC)
                v_sq = cAC.n
                w_mm = [(cAC, v_sq), (cGP, v_onesc)]
                if kc == 0:
                    w_mm += list(extra_first_mm_waits)
                sch.add("tensor", lambda e, kc=kc, h=h: e.matmul(
                    pg[0:1, 0:T], onesc[:], xbc[:, h * T:(h + 1) * T],
                    start=(kc == 0), stop=(kc == KC - 1)), waits=w_mm)
                sch.add("tensor", lambda e, kc=kc, h=h: e.matmul(
                    pu[0:1, 0:T], onesc[:], xsqc[:, h * T:(h + 1) * T],
                    start=(kc == 0), stop=(kc == KC - 1)), inc=cPE)
                stats_mm_after[kc] = cPE.n
            v_sums = cPE.n
            return _row_math(pg[0:1, 0:T], pu[0:1, 0:T], v_sums, cPE)

        def _row_math(sum_ap, sq_ap, v_sums, sums_ctr):
            sch.add("vector", lambda e: e.tensor_scalar(
                srowA[0:1, :], sum_ap, 1.0 / D, 0.0, ALU.mult, ALU.add),
                waits=[(sums_ctr, v_sums)], inc=cDV)
            sch.add("vector", lambda e: e.tensor_scalar(
                srowA[32:33, :], sq_ap, 1.0 / D, LN_EPS, ALU.mult, ALU.add), inc=cDV)
            sch.add("vector", lambda e: e.tensor_tensor(
                srowB[32:33, :], srowA[0:1, :], srowA[0:1, :], ALU.mult), inc=cDV)
            sch.add("vector", lambda e: e.tensor_tensor(
                srowA[32:33, :], srowA[32:33, :], srowB[32:33, :], ALU.subtract), inc=cDV)
            v_var = cDV.n
            sch.add("scalar", lambda e: e.activation(
                out=srowA[64:65, :], in_=srowA[32:33, :], func=AF.Ln),
                waits=[(cDV, v_var)], inc=cAC)
            sch.add("scalar", lambda e: e.activation(
                out=srowB[0:1, :], in_=srowA[64:65, :], func=AF.Exp, scale=-0.5),
                inc=cAC)
            v_rstd = cAC.n
            sch.add("vector", lambda e: e.tensor_tensor(
                srowB[32:33, :], srowA[0:1, :], srowB[0:1, :], ALU.mult),
                waits=[(cAC, v_rstd)], inc=cDV)
            sch.add("vector", lambda e: e.tensor_copy(srr_t[0:1, :], srowB[0:1, :]), inc=cDV)
            v_srr = cDV.n
            sch.add("tensor", lambda e: e.matmul(
                pg[:, 0:T], onesr[:], srr_t[0:1, :], start=True, stop=True),
                waits=[(cDV, v_srr)], inc=cPE)
            v_bc1 = cPE.n
            sch.add("vector", lambda e: e.tensor_copy(srr_t[0:1, :], srowB[32:33, :]),
                    waits=[(cPE, v_bc1)], inc=cDV)
            v_srr2 = cDV.n
            sch.add("tensor", lambda e: e.matmul(
                pu[:, 0:T], onesr[:], srr_t[0:1, :], start=True, stop=True),
                waits=[(cDV, v_srr2)], inc=cPE)
            v_bcmm = cPE.n
            sch.add("vector", lambda e: e.tensor_copy(bcA[:], pg[:, 0:T]),
                    waits=[(cPE, v_bcmm)], inc=cDV)
            sch.add("vector", lambda e: e.tensor_copy(bcB[:], pu[:, 0:T]), inc=cDV)
            return cDV.n

        def make_xa(v_bc):
            for kc in range(KC):
                h = kc % 2
                sch.add("vector", lambda e, kc=kc, h=h: e.tensor_tensor(
                    tmp32[:, h * T:(h + 1) * T], xt[:, kc * T:(kc + 1) * T], bcA[:],
                    ALU.mult), waits=[(cDV, v_bc)] if kc == 0 else ())
                sch.add("vector", lambda e, kc=kc, h=h: e.tensor_tensor(
                    xa[:, kc * T:(kc + 1) * T], tmp32[:, h * T:(h + 1) * T], bcB[:],
                    ALU.subtract), inc=cDV)
            return cDV.n

        def ph_ffn(call, xt_free_sy):
            prev_ystore = None
            for t in range(NTILE):
                for kc in range(KC):
                    w = [(cXST, xt_free_sy)] if kc == 0 and xt_free_sy else []
                    sch.add("sync", lambda e, kc=kc, t=t: e.dma_start(
                        out=xt[:, kc * T:(kc + 1) * T], in_=x_fm_ap(kc, t * T, T)),
                        waits=w, inc=cX, k=16)
                xt_free_sy = None
                chunk_ready = [cX.n] * KC
                v_bc = ln_stats_xa(chunk_ready)
                v_xa = make_xa(v_bc)
                if debug_outputs and call == 0 and t == 0:
                    sch.add("sync", lambda e: e.dma_start(out=dbg_xa[:], in_=xa[:]),
                            waits=[(cDV, v_xa)], inc=cSY, k=16)
                ev_after = {}
                sil_after = {}
                for ic in range(NI):
                    h = ic % 2
                    if call == 0 and t == 0:
                        emit_recon(2)
                    w_ld = []
                    if call == 0 and t == 0:
                        w_ld.append(recon_store[("g", 2 * ic)])
                        w_ld.append(recon_store[("g", 2 * ic + 1)])
                    if ic >= 2:
                        w_ld.append((cDV, ev_after[ic - 2]))
                    sch.add("sync", lambda e, ic=ic, h=h: e.dma_start(
                        out=gwb[:, h * D:(h + 1) * D], in_=w2g[:, ic * D:(ic + 1) * D]),
                        waits=w_ld, inc=cGW[h], k=16)
                    v_gw = cGW[h].n
                    w_ld2 = []
                    if call == 0 and t == 0:
                        w_ld2.append(recon_store[("u", 2 * ic)])
                        w_ld2.append(recon_store[("u", 2 * ic + 1)])
                    sch.add("sync", lambda e, ic=ic, h=h: e.dma_start(
                        out=uwb[:, h * D:(h + 1) * D], in_=w2u[:, ic * D:(ic + 1) * D]),
                        waits=w_ld2, inc=cUW[h], k=16)
                    v_uw = cUW[h].n
                    w_mm = [(cGW[h], v_gw), (cUW[h], v_uw), (cDV, v_xa)]
                    if ic >= 2:
                        w_mm.append((cDV, ev_after[ic - 2]))
                        w_mm.append((cAC, sil_after[ic - 2]))
                    for k in range(KC):
                        sch.add("tensor", lambda e, k=k, h=h: e.matmul(
                            pg[:, h * T:(h + 1) * T],
                            gwb[:, h * D + k * 128: h * D + (k + 1) * 128],
                            xa[:, k * T:(k + 1) * T], start=(k == 0), stop=(k == KC - 1)),
                            waits=w_mm if k == 0 else ())
                    if call == 0 and t == 0:
                        emit_recon(2)
                    for k in range(KC):
                        sch.add("tensor", lambda e, k=k, h=h: e.matmul(
                            pu[:, h * T:(h + 1) * T],
                            uwb[:, h * D + k * 128: h * D + (k + 1) * 128],
                            xa[:, k * T:(k + 1) * T], start=(k == 0), stop=(k == KC - 1)),
                            inc=(cPE if k == KC - 1 else None))
                    v_mm = cPE.n
                    sch.add("scalar", lambda e, h=h: e.activation(
                        out=tmp16[:, h * T:(h + 1) * T], in_=pg[:, h * T:(h + 1) * T],
                        func=AF.Silu), waits=[(cPE, v_mm)], inc=cAC)
                    sil_after[ic] = cAC.n
                    sch.add("vector", lambda e, ic=ic, h=h: e.tensor_tensor(
                        big[:, ic * T:(ic + 1) * T], tmp16[:, h * T:(h + 1) * T],
                        pu[:, h * T:(h + 1) * T], ALU.mult),
                        waits=[(cAC, sil_after[ic])], inc=cDV)
                    ev_after[ic] = cDV.n
                v_inter = cDV.n
                if debug_outputs and call == 0 and t == 0:
                    sch.add("sync", lambda e: e.dma_start(out=dbg_int[:], in_=big[:]),
                            waits=[(cDV, v_inter)], inc=cSY, k=16)
                dn_mm_after = {}
                dn_ev = {}
                for hc in range(KC):
                    v_dl = {}

                    def dn_load(q):
                        gi = hc * 4 + q
                        w_ld = []
                        if call == 0 and t == 0:
                            w_ld.append(recon_store[("d", 2 * gi)])
                            w_ld.append(recon_store[("d", 2 * gi + 1)])
                        if dn_mm_after.get(gi - 2):
                            w_ld.append((cPE, dn_mm_after[gi - 2]))
                        sch.add("sync", lambda e, gi=gi, q=q: e.dma_start(
                            out=dwb[:, (q % 2) * D:(q % 2) * D + D],
                            in_=w2d[:, gi * D:(gi + 1) * D]), waits=w_ld, inc=cDW[q % 2], k=16)
                        v_dl[q] = (cDW[q % 2], cDW[q % 2].n)

                    def dn_mm(q):
                        w_mm = [v_dl[q]]
                        if q == 0:
                            w_mm.append((cDV, v_inter))
                            if hc >= 1:
                                w_mm.append((cDV, dn_ev[hc - 1]))
                        for k in range(KC):
                            sch.add("tensor", lambda e, q=q, k=k: e.matmul(
                                pdn, dwb[:, (q % 2) * D + k * 128:(q % 2) * D + (k + 1) * 128],
                                big[:, (q * KC + k) * T:(q * KC + k + 1) * T],
                                start=(q == 0 and k == 0), stop=(q == 3 and k == KC - 1)),
                                waits=w_mm if k == 0 else (),
                                inc=(cPE if k == KC - 1 else None))
                        dn_mm_after[hc * 4 + q] = cPE.n

                    if call == 0 and t == 0:
                        emit_recon(4)
                    dn_load(0)
                    dn_load(1)
                    dn_mm(0)
                    if call == 0 and t == 0:
                        emit_recon(4)
                    dn_load(2)
                    dn_mm(1)
                    dn_load(3)
                    dn_mm(2)
                    dn_mm(3)
                    v_dmm = cPE.n
                    if debug_outputs and call == 0 and t == 0 and hc == 0:
                        sch.add("vector", lambda e: e.tensor_copy(
                            tmp32[:, 0:T], pdn), waits=[(cPE, v_dmm)], inc=cDV)
                        sch.add("sync", lambda e: e.dma_start(out=dbg_dn[:], in_=tmp32[:, 0:T]),
                                waits=[(cDV, cDV.n)], inc=cSY, k=16)
                    sch.add("vector", lambda e, hc=hc: e.tensor_tensor(
                        xt[:, hc * T:(hc + 1) * T], pdn, xt[:, hc * T:(hc + 1) * T],
                        ALU.add), waits=[(cPE, v_dmm)], inc=cDV)
                    dn_ev[hc] = cDV.n
                    if call == 0:
                        sch.add("sync", lambda e, hc=hc, t=t: e.dma_start(
                            out=x_fm_ap(hc, t * T, T), in_=xt[:, hc * T:(hc + 1) * T]),
                            waits=[(cDV, dn_ev[hc])], inc=cXST, k=16)
                if call == 0:
                    xt_free_sy = cXST.n
                else:
                    v_done_dv = dn_ev[KC - 1]
                    out_dv = {}
                    ystore = {}
                    for tc in range(4):
                        for g4 in range(4):
                            rnd = tc * 4 + g4
                            pb = pkv[:, T + (rnd % 3) * T: T + (rnd % 3) * T + T]
                            w_tr = [(cDV, v_done_dv)]
                            if rnd >= 3:
                                w_tr.append((cDV, out_dv[rnd - 3]))
                            for j in range(4):
                                kc = g4 * 4 + j
                                sch.add("tensor", lambda e, pb=pb, kc=kc, j=j, tc=tc: e.transpose(
                                    pb[:, j * 128:(j + 1) * 128],
                                    xt[:, kc * T + tc * 128: kc * T + tc * 128 + 128],
                                    ident[:]),
                                    waits=w_tr if j == 0 else (),
                                    inc=(cPE if j == 3 else None))
                            v_tr = cPE.n
                            h2 = rnd % 2
                            w_cp = [(cPE, v_tr)]
                            if prev_ystore:
                                w_cp.append((cSY, prev_ystore))
                            if ystore.get(rnd - 1):
                                w_cp.append((cSY, ystore[rnd - 1]))
                            sch.add("vector", lambda e, pb=pb, h2=h2: e.tensor_copy(
                                tmp32[:, h2 * T:(h2 + 1) * T], pb), waits=w_cp, inc=cDV)
                            out_dv[rnd] = cDV.n
                            sch.add("sync", lambda e, t=t, tc=tc, g4=g4, h2=h2: e.dma_start(
                                out=yout[t * T + tc * 128: t * T + (tc + 1) * 128,
                                         g4 * T:(g4 + 1) * T],
                                in_=tmp32[:, h2 * T:(h2 + 1) * T]),
                                waits=[(cDV, out_dv[rnd])], inc=cSY, k=16)
                            ystore[rnd] = cSY.n
                    prev_ystore = cSY.n
                    xt_free_sy = cSY.n
            return xt_free_sy

        def ph_attnA(xt_free_in):
            qk_tile = big[:, 0:KC * T]
            qkT = big[:, KC * T:2 * KC * T]
            vT = big[:, 2 * KC * T:3 * KC * T]
            st = {"xt_free_pe": None, "last_ac": None, "prev_vT": None,
                  "prev_kv": None, "precast": v_precast}
            for t in range(NTILE):
                for kc in range(KC):
                    w = []
                    if kc == 0:
                        if st["xt_free_pe"]:
                            w.append((cPE, st["xt_free_pe"]))
                            w.append((cAC, st["last_ac"]))
                        if t == 0:
                            w.append((cXST, xt_free_in))
                    sch.add("sync", lambda e, kc=kc, t=t: e.dma_start(
                        out=xt[:, kc * T:(kc + 1) * T], in_=x_fm_ap(kc, t * T, T)),
                        waits=w, inc=cX, k=16)
                chunk_ready = [cX.n] * KC
                extra = [(cAC, st["last_ac"])] if st["last_ac"] else []
                v_bc = ln_stats_xa(chunk_ready, extra)
                v_xa = make_xa(v_bc)

                pstate = {"wb_free": {}, "psum_free": {}, "qk_e": {}, "u_spill": {}}

                def proj(w_src, out_fn, extra_ld_waits=()):
                    res = {}
                    for oc in range(KC):
                        h = oc % 2
                        w_ld = list(extra_ld_waits) if oc == 0 else []
                        if pstate["wb_free"].get(h):
                            w_ld.append((cPE, pstate["wb_free"][h]))
                        sch.add("sync", lambda e, oc=oc, h=h, ws=w_src: e.dma_start(
                            out=awb[:, h * D:(h + 1) * D], in_=ws[:, oc * D:(oc + 1) * D]),
                            waits=w_ld, inc=cAW[h], k=16)
                        v_w = cAW[h].n
                        w_mm = [(cAW[h], v_w), (cDV, v_xa)]
                        if pstate["psum_free"].get(h):
                            w_mm.append(pstate["psum_free"][h])
                        pp = pg if h == 0 else pu
                        for k in range(KC):
                            sch.add("tensor", lambda e, k=k, pp=pp, h=h: e.matmul(
                                pp[:, 0:T], awb[:, h * D + k * 128: h * D + (k + 1) * 128],
                                xa[:, k * T:(k + 1) * T], start=(k == 0),
                                stop=(k == KC - 1)),
                                waits=w_mm if k == 0 else (),
                                inc=(cPE if k == KC - 1 else None))
                        v_mm = cPE.n
                        pstate["wb_free"][h] = v_mm
                        out_fn(oc, pp, h, v_mm, res)
                    return res

                def qk_out(oc, pp, h, v_mm, res):
                    w_z = [(cPE, v_mm)]
                    if pstate["qk_e"].get(h):
                        w_z.append((cAC, pstate["qk_e"][h]))
                    sch.add("vector", lambda e, pp=pp, oc=oc, h=h: e.tensor_scalar(
                        zbuf[:, h * T:(h + 1) * T], pp[:, 0:T], bias_ap(0, oc), 0.0,
                        ALU.add, ALU.add), waits=w_z, inc=cDV)
                    v_z = cDV.n
                    pstate["psum_free"][h] = (cDV, v_z)
                    sch.add("scalar", lambda e, h=h: e.activation(
                        out=tmp16[:, h * T:(h + 1) * T], in_=zbuf[:, h * T:(h + 1) * T],
                        func=AF.Exp), waits=[(cDV, v_z)], inc=cAC)
                    v_e = cAC.n
                    pstate["qk_e"][h] = v_e
                    w_em1 = [(cAC, v_e)]
                    if st["prev_vT"]:
                        w_em1.append((cTR, st["prev_vT"]))
                    sch.add("vector", lambda e, oc=oc, h=h: e.tensor_scalar(
                        qk_tile[:, oc * T:(oc + 1) * T], tmp16[:, h * T:(h + 1) * T],
                        -1.0, 0.0, ALU.add, ALU.add), waits=w_em1, inc=cDV)
                    sch.add("vector", lambda e, h=h: e.tensor_scalar(
                        m8[:, h * T:(h + 1) * T], zbuf[:, h * T:(h + 1) * T], 0.0, 0.0,
                        ALU.is_gt, ALU.bypass), inc=cDV)
                    sch.add("vector", lambda e, oc=oc, h=h: e.copy_predicated(
                        qk_tile[:, oc * T:(oc + 1) * T], m8[:, h * T:(h + 1) * T],
                        zbuf[:, h * T:(h + 1) * T]), inc=cDV)
                    v_q = cDV.n
                    res[oc] = v_q
                    sch.add("scalar", lambda e, oc=oc, t=t: e.dma_start(
                        out=qk_fm_ap(oc, t * T, T), in_=qk_tile[:, oc * T:(oc + 1) * T]),
                        waits=[(cDV, v_q)], inc=cSPL, k=16)

                extra0 = [(cPC, st["precast"])] if st["precast"] else []
                st["precast"] = None
                qk_done = proj(wqk_b, qk_out, extra0)
                v_qk_dv = cDV.n
                for hh in range(HEADS):
                    w_tr = [(cDV, qk_done[hh])]
                    if st["prev_kv"]:
                        w_tr.append((cPE, st["prev_kv"]))
                    sch.add("scalar", lambda e, hh=hh, t=t: e.dma_start_transpose(
                        qkT.rearrange("p (tc f) -> p tc f", tc=4)[:, :, hh * 128:(hh + 1) * 128],
                        qk_tile[:, hh * T:(hh + 1) * T]), waits=w_tr, inc=cTR, k=16)
                v_qkT = cTR.n

                def v_out(oc, pp, h, v_mm, res):
                    w_vs = [(cPE, v_mm), (cTR, v_qkT)]
                    if oc == 0:
                        w_vs.append((cSPL, cSPL.n))
                    sch.add("scalar", lambda e, pp=pp, oc=oc: e.activation(
                        out=qk_tile[:, oc * T:(oc + 1) * T], in_=pp[:, 0:T], func=AF.Silu,
                        bias=bias_ap(1, oc), scale=1.0),
                        waits=w_vs, inc=cAC)
                    v_v = cAC.n
                    pstate["psum_free"][h] = (cAC, v_v)
                    res[oc] = v_v

                v_done = proj(wv_b, v_out)
                for hh in range(HEADS):
                    w_tr = [(cAC, v_done[hh])]
                    if st["prev_kv"]:
                        w_tr.append((cPE, st["prev_kv"]))
                    sch.add("scalar", lambda e, hh=hh: e.dma_start_transpose(
                        vT.rearrange("p (tc f) -> p tc f", tc=4)[:, :, hh * 128:(hh + 1) * 128],
                        qk_tile[:, hh * T:(hh + 1) * T]), waits=w_tr, inc=cTR, k=16)
                v_vT = cTR.n

                def u_out(oc, pp, h, v_mm, res):
                    w_si = [(cPE, v_mm), (cDV, v_qk_dv)]
                    if pstate["u_spill"].get(h):
                        w_si.append((cSPL, cSPL.n))
                    sch.add("scalar", lambda e, pp=pp, oc=oc, h=h: e.activation(
                        out=tmp16[:, h * T:(h + 1) * T], in_=pp[:, 0:T], func=AF.Silu,
                        bias=bias_ap(2, oc), scale=1.0), waits=w_si, inc=cAC)
                    v_u = cAC.n
                    pstate["psum_free"][h] = (cAC, v_u)
                    sch.add("scalar", lambda e, h=h, oc=oc, t=t: e.dma_start(
                        out=u_fm_ap(oc, t * T, T), in_=tmp16[:, h * T:(h + 1) * T]),
                        waits=[(cAC, v_u)], inc=cSPL, k=16)
                    pstate["u_spill"][h] = (cSPL, cSPL.n)
                    res[oc] = v_u

                proj(wu_b, u_out)
                st["last_ac"] = cAC.n

                for tc in range(4):
                    for hh in range(HEADS):
                        first = (t == 0 and tc == 0 and hh % 4 == 0)
                        last = (t == NTILE - 1 and tc == 3)
                        w = [(cTR, v_vT)] if (hh == 0 and tc == 0) else ()
                        sch.add("tensor", lambda e, tc=tc, hh=hh, first=first, last=last: e.matmul(
                            pkv[:, hh * 128:(hh + 1) * 128],
                            qkT[:, tc * D + hh * 128: tc * D + (hh + 1) * 128],
                            vT[:, tc * D + hh * 128: tc * D + (hh + 1) * 128],
                            start=first, stop=last), waits=w,
                            inc=(cPE if (tc == 3 and hh == HEADS - 1) else None))
                st["prev_kv"] = cPE.n
                st["prev_vT"] = v_vT
                st["xt_free_pe"] = st["prev_kv"]
            return st["prev_kv"], st["last_ac"], cTR.n

        def ph_cc(v_kv_mm):
            kvf = xt[:, 0:HEADS * 128]
            sch.add("vector", lambda e, kvf=kvf: e.tensor_copy(kvf, pkv[:]),
                    waits=[(cPE, v_kv_mm)], inc=cDV)
            v_c = cDV.n
            sch.add("gpsimd", lambda e, kvf=kvf: e.dma_start(out=kv_in[:], in_=kvf),
                    waits=[(cDV, v_c)], inc=cKV, k=16)
            v_st = cKV.n
            sch.add("gpsimd", lambda e: e.collective_compute(
                "AllReduce", ALU.add, replica_groups=[list(range(NCORE))],
                ins=[kv_in[:].opt()], outs=[kv_out[:].opt()]),
                waits=[(cKV, v_st)], inc=cCC, k=1)
            v_cc = cCC.n
            sch.add("gpsimd", lambda e: e.dma_start(out=kvsb[:], in_=kv_out[:]),
                    waits=[(cCC, v_cc)], inc=cKV, k=16)
            v_ready = cKV.n
            if debug_outputs:
                sch.add("gpsimd", lambda e: e.dma_start(out=dbg_kv[:], in_=kv_out[:]),
                        waits=[(cKV, v_ready)], inc=cKV, k=16)
                sch.add("gpsimd", lambda e: e.dma_start(out=dbg_kvp[:], in_=kv_in[:]),
                        waits=[(cKV, v_ready)], inc=cKV, k=16)
            return v_ready, v_c

        def ph_attnB(v_kv_ready, xt_free_sy, v_attnA_sc, v_lastacA, o1_prev_pe, v_spills):
            o1 = big[:, 0:KC * T]
            o1sq = big[:, KC * T:2 * KC * T]
            ao = big[:, 2 * KC * T:3 * KC * T]
            st = {"o1mm": {}, "o1ev": {}, "womm": {}, "woev": {}, "aomul": {}, "o2": {}}
            for t in range(NTILE):
                for kc in range(KC):
                    w = []
                    if kc == 0:
                        if xt_free_sy:
                            w.append((cXST, xt_free_sy))
                        if t == 0:
                            w.append((cKV, v_kv_ready))
                            w.append((cPE, o1_prev_pe))
                    sch.add("sync", lambda e, kc=kc, t=t: e.dma_start(
                        out=xt[:, kc * T:(kc + 1) * T], in_=x_fm_ap(kc, t * T, T)),
                        waits=w, inc=cX, k=16)
                v_xtile = cX.n
                xt_free_sy = None
                for hh in range(HEADS):
                    h = hh % 2
                    w_ld = []
                    if st["o1mm"].get(hh - 2):
                        w_ld.append((cPE, st["o1mm"][hh - 2]))
                    if t == 0 and hh < 2:
                        w_ld.append((cSPL, v_spills))
                    sch.add("sync", lambda e, hh=hh, h=h, t=t: e.dma_start(
                        out=qch[:, h * T:(h + 1) * T], in_=qk_fm_ap(hh, t * T, T)),
                        waits=w_ld, inc=cQC[h], k=16)
                    v_q = cQC[h].n
                    pp = pg if h == 0 else pu
                    w_mm = [(cQC[h], v_q)]
                    if t == 0 and hh == 0:
                        w_mm.append((cGP, v_kv_ready))
                        w_mm.append((cAC, v_lastacA))
                    if st["o1ev"].get(hh - 2):
                        w_mm.append((cAC, st["o1ev"][hh - 2]))
                    if t > 0 and hh < 2 and st["womm"].get(KC - 2 + hh):
                        w_mm.append((cDV, st["woev"][KC - 2 + hh]))
                    sch.add("tensor", lambda e, hh=hh, h=h, pp=pp: e.matmul(
                        pp[:, 0:T], kvsb[:, hh * 128:(hh + 1) * 128],
                        qch[:, h * T:(h + 1) * T], start=True, stop=True),
                        waits=w_mm, inc=cPE)
                    st["o1mm"][hh] = cPE.n
                    v_mm = cPE.n
                    w_cp = [(cPE, v_mm)]
                    if t == 0 and hh == 0:
                        w_cp.append((cTR, v_attnA_sc))
                    sch.add("scalar", lambda e, hh=hh, pp=pp: e.copy(
                        o1[:, hh * T:(hh + 1) * T], pp[:, 0:T]),
                        waits=w_cp, inc=cAC)
                    sch.add("scalar", lambda e, hh=hh: e.activation(
                        out=o1sq[:, hh * T:(hh + 1) * T], in_=o1[:, hh * T:(hh + 1) * T],
                        func=AF.Square), inc=cAC)
                    st["o1ev"][hh] = cAC.n
                    v_ev = cAC.n
                    sch.add("tensor", lambda e, hh=hh: e.matmul(
                        pdn[0:1, :], onesc[:], o1[:, hh * T:(hh + 1) * T],
                        start=(hh == 0), stop=(hh == HEADS - 1)), waits=[(cAC, v_ev)])
                    sch.add("tensor", lambda e, hh=hh: e.matmul(
                        psq[0:1, :], onesc[:], o1sq[:, hh * T:(hh + 1) * T],
                        start=(hh == 0), stop=(hh == HEADS - 1)),
                        inc=(cPE if hh == HEADS - 1 else None))
                v_sums = cPE.n
                v_bc = _row_math(pdn[0:1, :], psq[0:1, :], v_sums, cPE)
                for hh in range(HEADS):
                    h = hh % 2
                    w_u = []
                    if st["aomul"].get(hh - 2):
                        w_u.append((cDV, st["aomul"][hh - 2]))
                    if t == 0 and hh < 2:
                        w_u.append((cSPL, v_spills))
                    sch.add("sync", lambda e, hh=hh, h=h, t=t: e.dma_start(
                        out=uch[:, h * T:(h + 1) * T], in_=u_fm_ap(hh, t * T, T)),
                        waits=w_u, inc=cUC[h], k=16)
                    v_u = cUC[h].n
                    w_ap = [(cDV, v_bc)] if hh == 0 else []
                    if st["o2"].get(hh - 2):
                        w_ap.append((cAC, st["o2"][hh - 2]))
                    sch.add("vector", lambda e, hh=hh, h=h: e.tensor_tensor(
                        tmp32[:, h * T:(h + 1) * T], o1[:, hh * T:(hh + 1) * T], bcA[:],
                        ALU.mult), waits=w_ap)
                    sch.add("vector", lambda e, h=h: e.tensor_tensor(
                        tmp32[:, h * T:(h + 1) * T], tmp32[:, h * T:(h + 1) * T], bcB[:],
                        ALU.subtract), inc=cDV)
                    v_n = cDV.n
                    sch.add("scalar", lambda e, hh=hh, h=h: e.activation(
                        out=tmp16[:, h * T:(h + 1) * T], in_=tmp32[:, h * T:(h + 1) * T],
                        func=AF.Identity, scale=bias_ap(4, hh), bias=bias_ap(5, hh)),
                        waits=[(cDV, v_n)], inc=cAC)
                    v_o2 = cAC.n
                    st["o2"][hh] = v_o2
                    sch.add("vector", lambda e, hh=hh, h=h: e.tensor_tensor(
                        ao[:, hh * T:(hh + 1) * T], tmp16[:, h * T:(h + 1) * T],
                        uch[:, h * T:(h + 1) * T], ALU.mult),
                        waits=[(cAC, v_o2), (cUC[h], v_u)], inc=cDV)
                    st["aomul"][hh] = cDV.n
                v_ao = cDV.n
                for oc in range(KC):
                    h = oc % 2
                    w_ld = []
                    if st["womm"].get(oc - 2):
                        w_ld.append((cPE, st["womm"][oc - 2]))
                    sch.add("sync", lambda e, oc=oc, h=h: e.dma_start(
                        out=awb[:, h * D:(h + 1) * D], in_=wo_b[:, oc * D:(oc + 1) * D]),
                        waits=w_ld, inc=cAW[h], k=16)
                    v_w = cAW[h].n
                    pp = pg if h == 0 else pu
                    w_mm = [(cAW[h], v_w), (cDV, v_ao)]
                    if st["woev"].get(oc - 2):
                        w_mm.append((cDV, st["woev"][oc - 2]))
                    for k in range(KC):
                        sch.add("tensor", lambda e, k=k, h=h, pp=pp: e.matmul(
                            pp[:, 0:T], awb[:, h * D + k * 128: h * D + (k + 1) * 128],
                            ao[:, k * T:(k + 1) * T], start=(k == 0), stop=(k == KC - 1)),
                            waits=w_mm if k == 0 else (),
                            inc=(cPE if k == KC - 1 else None))
                    st["womm"][oc] = cPE.n
                    v_mm = cPE.n
                    sch.add("scalar", lambda e, oc=oc, h=h, pp=pp: e.activation(
                        out=tmp32[:, h * T:(h + 1) * T], in_=pp[:, 0:T], func=AF.Identity,
                        scale=1.0, bias=bias_ap(3, oc)), waits=[(cPE, v_mm)], inc=cAC)
                    v_b = cAC.n
                    sch.add("vector", lambda e, oc=oc, h=h: e.tensor_tensor(
                        xt[:, oc * T:(oc + 1) * T], tmp32[:, h * T:(h + 1) * T],
                        xt[:, oc * T:(oc + 1) * T], ALU.add),
                        waits=[(cAC, v_b), (cX, v_xtile)], inc=cDV)
                    st["woev"][oc] = cDV.n
                    sch.add("sync", lambda e, oc=oc, t=t: e.dma_start(
                        out=x_fm_ap(oc, t * T, T), in_=xt[:, oc * T:(oc + 1) * T]),
                        waits=[(cDV, st["woev"][oc])], inc=cXST, k=16)
                xt_free_sy = cXST.n
            return xt_free_sy

        xt_free = ph_ffn(0, v_pre_xst)
        if debug_outputs:
            sch.add("sync", lambda e: e.dma_start(out=dbg1[:], in_=x_fm[:]),
                    waits=[(cXST, xt_free)], inc=cXST, k=16)
            xt_free = cXST.n
        v_kv, v_lastac, v_attnA_sc = ph_attnA(xt_free)
        v_kv_ready, _ = ph_cc(v_kv)
        xt_free = ph_attnB(v_kv_ready, xt_free, v_attnA_sc, v_lastac, v_kv, cSPL.n)
        if debug_outputs:
            sch.add("sync", lambda e: e.dma_start(out=dbg2[:], in_=x_fm[:]),
                    waits=[(cXST, xt_free)], inc=cXST, k=16)
            xt_free = cXST.n
        ph_ffn(1, xt_free)

        _all_ctrs = [cSY, cSC, cGP, cPE, cDV, cAC, cCC, cX, cXST, cSTG, cMSK,
                     cPC, cSPL, cTR, cKV] + cGW + cUW + cDW + cAW + cQC + cUC + cW2
        for c in _all_ctrs:
            if c.n > 0:
                sch.add("sync", lambda e: e.nop(), waits=[(c, c.n)])

        @block.tensor
        def _(e):
            sch.replay("tensor", e, sems)

        @block.vector
        def _(e):
            sch.replay("vector", e, sems)

        @block.scalar
        def _(e):
            sch.replay("scalar", e, sems)

        @block.sync
        def _(e):
            sch.replay("sync", e, sems)

        @block.gpsimd
        def _(e):
            sch.replay("gpsimd", e, sems)

    return nc


def _prep_inputs(inputs):
    tb = _tables()
    x = np.ascontiguousarray(np.asarray(inputs["L0_attn_x"], np.float32).reshape(NTOK, D))
    bias_m = np.zeros((128, 6 * 16), np.float32)
    for i, name in enumerate(["L0_attn_bqk", "L0_attn_bv", "L0_attn_bu", "L0_attn_bo",
                              "L0_attn_ln_g", "L0_attn_ln_b"]):
        bias_m[:, i * 16:(i + 1) * 16] = np.asarray(inputs[name], np.float32).reshape(16, 128).T
    basis_m = np.zeros((32, 512), np.float32)
    basis_m[0:20, 0:128] = tb["UA32"]
    basis_m[0:20, 128:256] = tb["UB32"]
    basis_m[0:20, 256:384] = 0.5 * tb["UA32"]
    basis_m[0:20, 384:512] = 0.5 * tb["UB32"]
    shared = dict(
        wqk=np.ascontiguousarray(np.asarray(inputs["L0_attn_Wqk"], np.float32)),
        wv=np.ascontiguousarray(np.asarray(inputs["L0_attn_Wv"], np.float32)),
        wu=np.ascontiguousarray(np.asarray(inputs["L0_attn_Wu"], np.float32)),
        wo=np.ascontiguousarray(np.asarray(inputs["L0_attn_Wo"], np.float32)),
        biases=bias_m,
        cptg=_cpt(np.asarray(inputs["L0_ffn_gate_cp"], np.float32), tb["b_of_gu"]),
        cptu=_cpt(np.asarray(inputs["L0_ffn_up_cp"], np.float32), tb["b_of_gu"]),
        cptd=_cpt(np.asarray(inputs["L0_ffn_down_cp"], np.float32), tb["b_of_dn"]),
        phig=tb["PHI_gu"], phid=tb["PHI_dn"],
        mskg=tb["MSK_gu"], mskd=tb["MSK_dn"],
        basis=basis_m,
        onesr_d=np.ones((1, 128), np.float32),
    )
    in_maps = []
    for c in range(NCORE):
        m = dict(shared)
        m["x"] = np.ascontiguousarray(x[c * PTOK:(c + 1) * PTOK])
        in_maps.append(m)
    return in_maps


def kernel(**inputs):
    in_maps = _prep_inputs(inputs)
    nc = _build()
    res = run_bass_kernel_spmd(nc, in_maps, core_ids=list(range(NCORE)))
    out = np.concatenate([np.asarray(res.results[c]["y"]) for c in range(NCORE)], axis=0)
    return out.reshape(1, NTOK, D).astype(np.float32)
